# version 06
# baseline (speedup 1.0000x reference)
import contextlib
import numpy as np
import ml_dtypes
import concourse.bass as bass
import concourse.mybir as mybir
from concourse.bass_utils import run_bass_kernel_spmd

F32 = mybir.dt.float32
BF16 = mybir.dt.bfloat16
AF = mybir.ActivationFunctionType
ALU = mybir.AluOpType
AX = mybir.AxisListType

NCORE = 8
T = 8192
TC = T // NCORE
D = 2048
DC = D // 128
DI = 4096
DIC = DI // 128
NST = 16
NEXP = 8
HES = 11
MSHC = 12

_DEBUG_TAPS = set()
_PHASE_BARRIERS = {"all"}


class _Sem:
    def __init__(self, es, nc, name):
        self.h = es.enter_context(nc.semaphore(name))
        self.n = 0
        self.name = name

    def t(self):
        return (self, self.n)


class _Prog:
    ENGS = ("sync", "vector", "scalar", "tensor", "gpsimd")

    def __init__(self):
        self.streams = {e: [] for e in self.ENGS}
        self.waited = {e: {} for e in self.ENGS}

    def o(self, eng, fn, inc=None):
        if inc:
            def wrapped(e, fn=fn, inc=tuple(inc)):
                ins = fn(e)
                for sem, k in inc:
                    ins = ins.then_inc(sem.h, k)
                return ins
            self.streams[eng].append(wrapped)
            for sem, k in inc:
                sem.n += k
        else:
            self.streams[eng].append(fn)

    def w(self, eng, sem, val=None):
        if val is None:
            val = sem.n
        if val <= 0 or self.waited[eng].get(sem.name, -1) >= val:
            return
        self.waited[eng][sem.name] = val
        self.streams[eng].append(lambda e, s=sem.h, v=val: e.wait_ge(s, v))

    def wt(self, eng, ticket):
        if ticket is not None:
            self.w(eng, ticket[0], ticket[1])


def _build():
    nc = bass.Bass("TRN2", target_bir_lowering=False)

    def din(name, shape, dt=F32):
        return nc.dram_tensor(name, list(shape), dt, kind="ExternalInput").ap()

    xt = din("xt", [D, TC])
    wmod = din("wmod", [2 * MSHC, 128, D])
    win = din("win", [DC, 128, D])
    wgate = din("wgate", [DC, 128, D])
    sgw = din("sgw", [4 * HES, 128, D])
    sdw = din("sdw", [2 * DC, 128, 1408])
    upw = din("upw", [NEXP * 2 * HES, 128, D])
    dnw = din("dnw", [NEXP * DC, 128, 1408])
    inpj = din("inpj", [2 * DIC, 128, D])
    xpja = din("xpja", [2, 128, D])
    xpjb = din("xpjb", [2, 128, 512])
    dtpj = din("dtpj", [DIC, 128, 128])
    outpj = din("outpj", [2 * DC, 128, D])
    fw = din("fw", [DC, 128, D])
    gw = din("gw", [128, DC * NEXP])
    cv0 = din("cv0", [128, DC])
    cv1 = din("cv1", [128, DC])
    flP = din("flP", [128, DC])
    binP = din("binP", [128, DC])
    bgP = din("bgP", [128, DC])
    cwP = din("cwP", [128, DIC * 4])
    cbP = din("cbP", [128, DIC])
    dtbP = din("dtbP", [128, DIC])
    AT = din("AT", [128, 512])
    DPi = din("DPi", [128, DIC])
    fbP = din("fbP", [128, DC])
    biasT = din("biasT", [128, NEXP])
    ident = din("ident", [128, 128])
    onesr = din("onesr", [1, 128])
    onescb = din("onescb", [128, 1], BF16)
    onescf = din("onescf", [128, 1])
    consts = din("consts", [128, 8])
    selown = din("selown", [128, NCORE])
    selprev = din("selprev", [128, NCORE])

    yo = nc.dram_tensor("yo", [D, TC], F32, kind="ExternalOutput").ap()

    def dint(name, shape, dt=F32):
        return nc.dram_tensor(name, list(shape), dt, kind="Internal").ap()

    cc_mod_i = dint("cc_mod_i", [128, 2 * MSHC])
    cc_mod_o = dint("cc_mod_o", [128 * NCORE, 2 * MSHC])
    cc_l0_i = dint("cc_l0_i", [128, 2 * DC])
    cc_l0_o = dint("cc_l0_o", [128 * NCORE, 2 * DC])
    cc_h_i = dint("cc_h_i", [128, 3 * DIC])
    cc_h_o = dint("cc_h_o", [128 * NCORE, 3 * DIC])
    cc_m_i = dint("cc_m_i", [128, 2 * DIC * NST])
    cc_m_o = dint("cc_m_o", [128 * NCORE, 2 * DIC * NST])
    gate_sp = dint("gate_sp", [DIC, 128, TC], BF16)
    bc_dram = dint("bc_dram", [32, TC])
    bcb_dram = dint("bcb_dram", [32, TC], BF16)
    we_dram = dint("we_dram", [NEXP, TC])
    x2_sp = dint("x2_sp", [DC, 128, TC])
    y_sp = dint("y_sp", [DIC, 128, TC])

    es = contextlib.ExitStack()

    def sbuf(name, p, f, dt=F32):
        return es.enter_context(nc.sbuf_tensor(name, [p, f], dt))

    E = sbuf("arE", 128, 16384)
    F = sbuf("arF", 128, 16384)
    Eb, Fb = E.bitcast(BF16), F.bitcast(BF16)
    STGF = sbuf("stgf", 128, 2048)
    STGB = [sbuf("stgb0", 128, 2048, BF16), sbuf("stgb1", 128, 2048, BF16)]
    SCR = sbuf("scr", 128, 5 * 1024)
    SCRb = SCR.bitcast(BF16)
    ROWS = SCR[0:1, 0:3072]
    YG = sbuf("yg", 128, 4096)
    BCr = sbuf("bcr", 32, TC)
    BCb = sbuf("bcb", 32, TC, BF16)
    R8 = sbuf("r8", 128, 48)
    SM = sbuf("sm", 128, 3072)
    SMC = sbuf("smc", 128, 8)
    IDENT = sbuf("identS", 128, 128)
    ONESR = sbuf("onesrS", 1, 128)
    ONECB = sbuf("onecbS", 128, 1, BF16)
    ONECF = sbuf("onecfS", 128, 1)
    GWT = sbuf("gwtS", 128, DC * NEXP)
    GWTB = sbuf("gwtbS", 128, DC * NEXP, BF16)
    BIAST = sbuf("biastS", 128, NEXP)
    SELO = sbuf("seloS", 128, NCORE)
    SELP = sbuf("selpS", 128, NCORE)
    BINP = sbuf("binpS", 128, DC)
    BGP = sbuf("bgpS", 128, DC)
    CW = sbuf("cwS", 128, DIC * 4)
    CB = sbuf("cbS", 128, DIC)
    DTB = sbuf("dtbS", 128, DIC)
    DPT = sbuf("dpS", 128, DIC)
    FBP = sbuf("fbpS", 128, DC)
    modsh = sbuf("modsh", 128, 2 * MSHC)
    SC8 = E[0:8, 5632:6656]
    WE8 = E[0:8, 6656:7680]

    OF = dict(scT0=0, scT1=16, mc0=32, mc1=128, ca10=224, ga0=240, cb10=256,
              gb0=272, ca11=288, ga1=304, cb11=320, gb1=336, sfl=352,
              gac=368, negA=384, ndtb=896, sdt=928, tA=960, tB=976,
              exA0=1024, exH0=1040, carry=1056, rgb0=1072, rgb1=1088,
              th=1104, comb=1120, pick=1136,
              exAm=1440, exHm=1952, carrym=2464)
    OF["agb"] = OF["exAm"]

    def sm(off, w=16):
        return SM[:, off: off + w]

    cZ, cONE, cEPS6, cEPS5 = SMC[:, 0:1], SMC[:, 1:2], SMC[:, 2:3], SMC[:, 3:4]

    sW = _Sem(es, nc, "sW")
    sPE = _Sem(es, nc, "sPE")
    sV = _Sem(es, nc, "sV")
    sS = _Sem(es, nc, "sS")
    sG = _Sem(es, nc, "sG")
    sO = _Sem(es, nc, "sO")
    ALLSEMS = [sW, sPE, sV, sS, sG, sO]
    P = _Prog()

    pb = [es.enter_context(nc.psum_tensor(f"pb{i}", [128, 512], F32)) for i in range(8)]

    def cell(buf, c, t0=0, wd=TC):
        return buf[:, c * TC + t0: c * TC + t0 + wd]

    def scr(slot, t0=0, wd=TC):
        return SCR[:, slot * TC + t0: slot * TC + t0 + wd]

    def scrb(slot, t0=0, wd=TC):
        return SCRb[:, 2 * slot * TC + t0: 2 * slot * TC + t0 + wd]

    def barrier():
        for e in _Prog.ENGS:
            for s in ALLSEMS:
                P.w(e, s)

    def pbar(name):
        if "all" in _PHASE_BARRIERS or name in _PHASE_BARRIERS:
            barrier()

    tapn = []

    def tap(name, ap_fn):
        if name not in _DEBUG_TAPS:
            return
        ap = ap_fn()
        if ap.dtype == BF16:
            ap = ap.bitcast(F32)
        o = nc.dram_tensor("tap_" + name, list(ap.shape), ap.dtype,
                           kind="ExternalOutput").ap()
        tapn.append("tap_" + name)
        barrier()
        nfree = ap.shape[-1]
        step = max(1, min(nfree, (2 * 1024 * 1024) // (128 * 4)))
        for f0 in range(0, nfree, step):
            f1 = min(nfree, f0 + step)
            P.o("gpsimd", lambda e, o=o, ap=ap, f0=f0, f1=f1: e.dma_start(
                out=o[:, f0:f1], in_=ap[:, f0:f1]), inc=[(sO, 16)])
        barrier()

    def V(fn, inc=True):
        P.o("vector", fn, inc=[(sV, 1)] if inc else None)

    def Sc(fn, inc=True):
        P.o("scalar", fn, inc=[(sS, 1)] if inc else None)

    def PE(fn, inc=True):
        P.o("tensor", fn, inc=[(sPE, 1)] if inc else None)

    def VD(fn):
        V(fn)
        P.w("vector", sV)

    ws = dict(idx=0, stg_free=None, stgb_free=[None, None])

    def stream(dram3, nstrips, use_fn, first=0, cast=True, mw=128, kdim=D,
               prep_fn=None):
        kc_n = kdim // 128
        for j in range(nstrips):
            par = ws["idx"] % 2
            ws["idx"] += 1
            P.wt("sync", ws["stg_free"])
            src = dram3[first + j]
            dst = STGF[:, : kc_n * mw]
            P.o("sync", lambda e, s=src, d=dst: e.dma_start(out=d, in_=s), inc=[(sW, 16)])
            load_t = sW.t()
            if prep_fn is not None:
                ready, buf = prep_fn(j, par, load_t)
                ws["stg_free"] = ready
            elif cast:
                P.wt("vector", load_t)
                P.wt("vector", ws["stgb_free"][par])
                V(lambda e, par=par, n=kc_n * mw: e.tensor_copy(
                    out=STGB[par][:, :n], in_=STGF[:, :n]))
                ready = sV.t()
                ws["stg_free"] = ready
                buf = STGB[par]
            else:
                ready = load_t
                buf = STGF

            def tile(kc, buf=buf, mw=mw):
                return buf[:, kc * mw:(kc + 1) * mw]
            pe_t = use_fn(j, tile, ready)
            if prep_fn is not None or cast:
                ws["stgb_free"][par] = pe_t
            else:
                ws["stg_free"] = pe_t

    bank_free = [None] * 8

    tiny_pairs = [
        (SMC[:, :], consts), (sm(OF["tA"]), cv0), (sm(OF["tB"]), cv1),
        (sm(OF["sfl"]), flP), (IDENT[:, :], ident), (ONESR[:, :], onesr),
        (ONECB[:, :], onescb), (ONECF[:, :], onescf), (GWT[:, :], gw),
        (BIAST[:, :], biasT), (SELO[:, :], selown), (SELP[:, :], selprev),
        (BINP[:, :], binP), (BGP[:, :], bgP), (CW[:, :], cwP), (CB[:, :], cbP),
        (DTB[:, :], dtbP), (DPT[:, :], DPi), (FBP[:, :], fbP),
        (SCR[:, 0:512], AT),
    ]
    for dst, src in tiny_pairs:
        P.o("sync", lambda e, d=dst, s=src: e.dma_start(out=d, in_=s), inc=[(sW, 16)])

    P.w("vector", sW)
    V(lambda e: e.tensor_copy(out=GWTB[:, :], in_=GWT[:, :]))
    V(lambda e: e.tensor_scalar(out=sm(OF["negA"], 512), in0=SCR[:, 0:512],
                                scalar1=-1.0, scalar2=None, op0=ALU.mult))
    V(lambda e: e.tensor_scalar(out=sm(OF["ndtb"], 32), in0=DTB[:, 0:32],
                                scalar1=-1.0, scalar2=None, op0=ALU.mult))
    P.w("scalar", sW)
    Sc(lambda e: e.activation(out=sm(OF["scT0"]), in_=sm(OF["tA"]), func=AF.Silu))
    Sc(lambda e: e.activation(out=sm(OF["scT1"]), in_=sm(OF["tB"]), func=AF.Silu))
    DBG = SM[:, 2976:3072]
    V(lambda e: e.tensor_scalar(out=DBG[:, 0:16], in0=sm(OF["sfl"]),
                                scalar1=17.0, scalar2=None, op0=ALU.min))
    P.w("scalar", sV)
    Sc(lambda e: e.activation(out=DBG[:, 16:32], in_=DBG[:, 0:16], func=AF.Exp))
    Sc(lambda e: e.activation(out=DBG[:, 32:48], in_=DBG[:, 16:32], func=AF.Ln,
                              bias=cONE, scale=1.0))
    P.w("vector", sS)
    VD(lambda e: e.tensor_tensor(out=DBG[:, 48:64], in0=DBG[:, 32:48],
                                 in1=sm(OF["sfl"]), op=ALU.max))
    VD(lambda e: e.tensor_scalar(out=sm(OF["sfl"]), in0=DBG[:, 48:64],
                                 scalar1=-8.0, scalar2=None, op0=ALU.mult))

    V(lambda e: e.memset(sm(OF["tA"], 1), 0.0))

    def fw1_use(j, tile, ready):
        P.wt("vector", ready)
        VD(lambda e: e.tensor_reduce(
            sm(OF["tB"], 1), STGF[:, 0:2048], axis=AX.X, op=ALU.add,
            apply_absolute_value=True))
        VD(lambda e: e.tensor_tensor(out=sm(OF["tA"], 1), in0=sm(OF["tA"], 1),
                                     in1=sm(OF["tB"], 1), op=ALU.add))
        return sV.t()

    stream(fw, DC, fw1_use, cast=False)
    P.w("tensor", sV)
    PE(lambda e: e.matmul(pb[6][0:1, 0:1], sm(OF["tA"], 1), ONECF[:, 0:1],
                          start=True, stop=True))
    P.w("vector", sPE)
    VD(lambda e: e.tensor_scalar(out=scr(1, 64, 1)[0:1, :], in0=pb[6][0:1, 0:1],
                                 scalar1=1.0 / (2048.0 * 2048.0), scalar2=None,
                                 op0=ALU.mult))
    VD(lambda e: e.tensor_scalar(out=scr(1, 65, 1)[0:1, :], in0=scr(1, 64, 1)[0:1, :],
                                 scalar1=0.5, scalar2=0.5e-10, op0=ALU.mult,
                                 op1=ALU.add))
    VD(lambda e: e.tensor_scalar(out=scr(1, 66, 1)[0:1, :], in0=scr(1, 65, 1)[0:1, :],
                                 scalar1=-1.0, scalar2=None, op0=ALU.mult))
    P.w("tensor", sV)
    PE(lambda e: e.matmul(pb[6][:, 0:3], ONESR[:, :], scr(1, 64, 3)[0:1, :],
                          start=True, stop=True))
    P.w("vector", sPE)
    V(lambda e: e.tensor_copy(out=sm(OF["th"], 3), in_=pb[6][:, 0:3]))
    bank_free[6] = sV.t()
    scP = sm(OF["th"], 3)[:, 0:1]
    thP = sm(OF["th"], 3)[:, 1:2]
    nthP = sm(OF["th"], 3)[:, 2:3]

    def mod_use(L):
        def use(j, tile, ready):
            P.wt("tensor", ready)
            P.w("tensor", sS)
            bank = 6 + (j % 2)
            P.wt("tensor", bank_free[bank])
            for kc in range(DC):
                PE(lambda e, kc=kc, tile=tile, bank=bank, L=L: e.matmul(
                    pb[bank][:, 0:1], tile(kc),
                    sm(OF["scT0"] if L == 0 else OF["scT1"])[:, kc:kc + 1],
                    start=(kc == 0), stop=(kc == DC - 1), skip_group_check=True),
                    inc=(kc == DC - 1))
            pe_t = sPE.t()
            P.wt("vector", pe_t)
            V(lambda e, j=j, bank=bank, L=L: e.tensor_copy(
                out=modsh[:, L * MSHC + j: L * MSHC + j + 1], in_=pb[bank][:, 0:1]))
            bank_free[bank] = sV.t()
            return pe_t
        return use

    stream(wmod, MSHC, mod_use(0), first=0, cast=False)
    stream(wmod, MSHC, mod_use(1), first=MSHC, cast=False)

    P.w("gpsimd", sV)
    P.o("gpsimd", lambda e: e.dma_start(out=cc_mod_i, in_=modsh[:, :]), inc=[(sG, 16)])
    P.w("gpsimd", sG)
    P.o("gpsimd", lambda e: e.collective_compute(
        "AllGather", ALU.bypass, replica_groups=[list(range(NCORE))],
        ins=[cc_mod_i], outs=[cc_mod_o]), inc=[(sG, 1)])
    P.w("gpsimd", sG)
    ccm3 = cc_mod_o.rearrange("(n p) f -> n p f", n=NCORE)
    for j in range(NCORE):
        P.o("gpsimd", lambda e, j=j: e.dma_start(
            out=sm(OF["mc0"], 96)[:, j * MSHC:(j + 1) * MSHC], in_=ccm3[j][:, 0:MSHC]),
            inc=[(sG, 16)])
        P.o("gpsimd", lambda e, j=j: e.dma_start(
            out=sm(OF["mc1"], 96)[:, j * MSHC:(j + 1) * MSHC],
            in_=ccm3[j][:, MSHC:2 * MSHC]), inc=[(sG, 16)])

    P.w("vector", sG)
    for L, (mc, ca1o, gao, cb1o, gbo) in enumerate([
            (OF["mc0"], OF["ca10"], OF["ga0"], OF["cb10"], OF["gb0"]),
            (OF["mc1"], OF["ca11"], OF["ga1"], OF["cb11"], OF["gb1"])]):
        mcol = sm(mc, 96)
        ca_, ga_, cb_, gb_ = mcol[:, 16:32], mcol[:, 32:48], mcol[:, 64:80], mcol[:, 80:96]
        V(lambda e, ca_=ca_, o=ca1o: e.tensor_scalar(
            out=sm(o), in0=ca_, scalar1=1.0, scalar2=None, op0=ALU.add))
        V(lambda e, cb_=cb_, o=cb1o: e.tensor_scalar(
            out=sm(o), in0=cb_, scalar1=1.0, scalar2=None, op0=ALU.add))
        for src_, dsto in ((ga_, gao), (gb_, gbo)):
            VD(lambda e, s=src_: e.tensor_scalar(
                out=scr(1, 0, 16), in0=s, scalar1=0.0, scalar2=None, op0=ALU.is_ge))
            VD(lambda e: e.tensor_scalar(
                out=scr(1, 32, 16), in0=scr(1, 0, 16), scalar1=2.0, scalar2=-1.0,
                op0=ALU.mult, op1=ALU.add))
            VD(lambda e, s=src_: e.tensor_scalar(
                out=scr(1, 16, 16), in0=s, scalar1=-1.0, scalar2=None, op0=ALU.mult))
            VD(lambda e, s=src_: e.tensor_tensor(
                out=scr(1, 48, 16), in0=scr(1, 16, 16), in1=s, op=ALU.max))
            VD(lambda e: e.tensor_scalar(
                out=scr(1, 16, 16), in0=scr(1, 48, 16), scalar1=0.25, scalar2=None,
                op0=ALU.max))
            VD(lambda e, o=dsto: e.tensor_tensor(
                out=sm(o), in0=scr(1, 32, 16), in1=scr(1, 16, 16), op=ALU.mult))
    VD(lambda e: e.reciprocal(out=sm(OF["rgb0"]), in_=sm(OF["gb0"])))
    VD(lambda e: e.reciprocal(out=sm(OF["rgb1"]), in_=sm(OF["gb1"])))
    tap("mod0", lambda: sm(OF["mc0"], 96))
    tap("gclamp", lambda: sm(OF["ca10"], 128))
    tap("mod1", lambda: sm(OF["mc1"], 96))
    tap("thsc", lambda: sm(OF["th"], 3))

    def ln_phase(src_fn, xb_fn, hb_fn, ca1_off, sa_ap, rows_tap=None):
        for bk in (4, 5, 6, 7):
            P.wt("tensor", bank_free[bk])
        for c in range(DC):
            src, tkt = src_fn(c)
            par = (c % 2) * TC
            P.wt("vector", tkt)
            V(lambda e, s=src, c=c: e.tensor_copy(out=xb_fn(c), in_=s))
            P.w("vector", sPE)
            V(lambda e, c=c, par=par: e.tensor_tensor(
                out=scrb(3, par, TC), in0=xb_fn(c), in1=xb_fn(c), op=ALU.mult))
            P.w("tensor", sV)
            for t2 in range(2):
                PE(lambda e, c=c, t2=t2: e.matmul(
                    pb[4 + t2][0:1, :], ONECB[:, 0:1],
                    xb_fn(c)[:, t2 * 512:(t2 + 1) * 512],
                    start=(c == 0), stop=(c == DC - 1),
                    skip_group_check=True), inc=(c == DC - 1))
                PE(lambda e, c=c, t2=t2, par=par: e.matmul(
                    pb[6 + t2][0:1, :], ONECB[:, 0:1],
                    scrb(3, par + t2 * 512, 512), start=(c == 0), stop=(c == DC - 1),
                    skip_group_check=True), inc=(c == DC - 1))
        P.w("vector", sPE)
        for t2 in range(2):
            V(lambda e, t2=t2: e.tensor_scalar(
                out=ROWS[:, t2 * 512:(t2 + 1) * 512], in0=pb[4 + t2][0:1, :],
                scalar1=1.0 / D, scalar2=None, op0=ALU.mult))
            V(lambda e, t2=t2: e.tensor_scalar(
                out=ROWS[:, 1024 + t2 * 512: 1024 + (t2 + 1) * 512],
                in0=pb[6 + t2][0:1, :], scalar1=1.0 / D, scalar2=None, op0=ALU.mult))
        V(lambda e: e.tensor_tensor(out=ROWS[:, 2048:3072], in0=ROWS[:, 0:1024],
                                    in1=ROWS[:, 0:1024], op=ALU.mult))
        V(lambda e: e.tensor_tensor(out=ROWS[:, 1024:2048], in0=ROWS[:, 1024:2048],
                                    in1=ROWS[:, 2048:3072], op=ALU.subtract))
        P.w("scalar", sV)
        Sc(lambda e: e.activation(out=ROWS[:, 2048:3072], in_=ROWS[:, 1024:2048],
                                  func=AF.Ln, bias=cEPS5[0:1, :], scale=1.0))
        Sc(lambda e: e.activation(out=ROWS[:, 1024:2048], in_=ROWS[:, 2048:3072],
                                  func=AF.Exp, scale=-0.5))
        P.w("vector", sS)
        if rows_tap:
            tap(rows_tap, lambda: SCR[0:1, 0:3072])
        V(lambda e: e.tensor_tensor(out=ROWS[:, 2048:3072], in0=ROWS[:, 0:1024],
                                    in1=ROWS[:, 1024:2048], op=ALU.mult))
        V(lambda e: e.tensor_scalar(out=ROWS[:, 2048:3072], in0=ROWS[:, 2048:3072],
                                    scalar1=-1.0, scalar2=None, op0=ALU.mult))
        P.w("tensor", sV)
        P.w("tensor", sS)
        for t2 in range(2):
            PE(lambda e, t2=t2: e.matmul(pb[4 + t2][:, :], ONESR[:, :],
                                         ROWS[:, 1024 + t2 * 512: 1024 + (t2 + 1) * 512],
                                         start=True, stop=True), inc=(t2 == 1))
            PE(lambda e, t2=t2: e.matmul(pb[6 + t2][:, :], ONESR[:, :],
                                         ROWS[:, 2048 + t2 * 512: 2048 + (t2 + 1) * 512],
                                         start=True, stop=True), inc=(t2 == 1))
        P.w("vector", sPE)
        for t2 in range(2):
            V(lambda e, t2=t2: e.tensor_copy(out=scr(0, t2 * 512, 512), in_=pb[4 + t2][:, :]))
            V(lambda e, t2=t2: e.tensor_copy(out=scr(1, t2 * 512, 512), in_=pb[6 + t2][:, :]))
        for bk in (4, 5, 6, 7):
            bank_free[bk] = sV.t()
        for c in range(DC):
            V(lambda e, c=c: e.tensor_tensor(
                out=scr(2), in0=xb_fn(c), in1=scr(0), op=ALU.mult))
            V(lambda e, c=c: e.tensor_tensor(
                out=scr(3), in0=scr(2), in1=scr(1), op=ALU.add))
            V(lambda e, c=c: e.tensor_scalar(
                out=hb_fn(c), in0=scr(3), scalar1=sm(ca1_off)[:, c:c + 1],
                scalar2=sa_ap[:, c:c + 1], op0=ALU.mult, op1=ALU.add))
        return sV.t()

    pbar("p0")
    def h0b_fn(c):
        return Fb[:, c * 2048: c * 2048 + 1024]

    def hloc_fn(c):
        return Fb[:, c * 2048 + 1024: c * 2048 + 2048]

    def x_src(c):
        P.w("sync", sV)
        P.o("sync", lambda e, c=c: e.dma_start(
            out=scr(4), in_=xt[c * 128:(c + 1) * 128, :]), inc=[(sW, 16)])
        return scr(4), sW.t()

    ln_phase(x_src, lambda c: cell(Eb, c), h0b_fn, OF["ca10"],
             sm(OF["mc0"], 96)[:, 0:16])
    tap("h0b", lambda: Fb[:, 0:2 * DC * TC])

    pbar("ln1")
    def gates_use(off, bias_tile):
        def use(ms, tile, ready):
            P.wt("tensor", ready)
            for t2 in range(2):
                bank = (ms % 2) * 2 + t2
                P.wt("tensor", bank_free[bank])
                for kc in range(DC):
                    PE(lambda e, kc=kc, t2=t2, bank=bank, tile=tile: e.matmul(
                        pb[bank][:, :], tile(kc), h0b_fn(kc)[:, t2 * 512:(t2 + 1) * 512],
                        start=(kc == 0), stop=(kc == DC - 1)), inc=(kc == DC - 1))
                pe_t = sPE.t()
                P.wt("scalar", pe_t)
                Sc(lambda e, ms=ms, t2=t2, bank=bank, off=off, bias_tile=bias_tile:
                   e.activation(out=cell(Eb, off + ms, t2 * 512, 512), in_=pb[bank][:, :],
                                func=AF.Sigmoid, bias=bias_tile[:, ms:ms + 1], scale=1.0))
                bank_free[bank] = sS.t()
            return sPE.t()
        return use

    stream(win, DC, gates_use(0, BINP))
    stream(wgate, DC, gates_use(DC, BGP))
    tap("sigig", lambda: Eb[:, 0: DC * TC])
    tap("sigrg", lambda: Eb[:, DC * TC: 2 * DC * TC])

    pbar("gates")
    for c in range(DC):
        P.w("scalar", sV)
        Sc(lambda e, c=c: e.activation(out=scr(0), in_=cell(Eb, DC + c), func=AF.Exp,
                                       scale=sm(OF["sfl"])[:, c:c + 1]))
        Sc(lambda e: e.activation(out=scr(1), in_=scr(0), func=AF.Square))
        Sc(lambda e: e.activation(out=scr(2), in_=scr(1), func=AF.Ln,
                                  bias=cEPS6, scale=-1.0))
        Sc(lambda e: e.activation(out=scr(1), in_=scr(2), func=AF.Exp, scale=0.5))
        P.w("vector", sS)
        V(lambda e, c=c: e.tensor_tensor(out=scr(2), in0=scr(1), in1=cell(Eb, c),
                                         op=ALU.mult))
        V(lambda e, c=c: e.tensor_tensor(out=scr(3), in0=scr(2), in1=h0b_fn(c),
                                         op=ALU.mult))
        V(lambda e: e.tensor_tensor_scan(out=scr(2), data0=scr(0), data1=scr(3),
                                         initial=cZ, op0=ALU.mult, op1=ALU.add))
        V(lambda e, c=c: e.tensor_copy(out=hloc_fn(c), in_=scr(2)))
        V(lambda e, c=c: e.tensor_copy(out=sm(OF["exH0"])[:, c:c + 1],
                                       in_=scr(2, 1023, 1)))
        V(lambda e: e.tensor_tensor_scan(out=scr(3), data0=scr(0), data1=scr(0),
                                         initial=cONE, op0=ALU.mult, op1=ALU.bypass))
        V(lambda e, c=c: e.tensor_copy(out=h0b_fn(c), in_=scr(3)))
        V(lambda e, c=c: e.tensor_copy(out=sm(OF["exA0"])[:, c:c + 1],
                                       in_=scr(3, 1023, 1)))
    tap("hloc0", lambda: Fb[:, 0: 2 * DC * TC])
    tap("sfl", lambda: sm(OF["sfl"], 16))
    tap("dbg", lambda: SM[:, 2976:3072])
    tap("alpha15", lambda: scr(0))
    tap("beta15", lambda: scr(1))

    pbar("scan0")
    def exchange(in_off, width, cc_i, cc_o, pick_off):
        P.w("gpsimd", sV)
        P.w("gpsimd", sS)
        P.o("gpsimd", lambda e: e.dma_start(out=cc_i, in_=sm(in_off, width)),
            inc=[(sG, 16)])
        P.w("gpsimd", sG)
        P.o("gpsimd", lambda e: e.collective_compute(
            "AllGather", ALU.bypass, replica_groups=[list(range(NCORE))],
            ins=[cc_i], outs=[cc_o]), inc=[(sG, 1)])
        cc3 = cc_o.rearrange("(n p) f -> n p f", n=NCORE)
        half = width // 2
        VD(lambda e, half=half, pick_off=pick_off: e.memset(sm(pick_off, half), 0.0))
        VD(lambda e, half=half: e.memset(scr(4, 0, half), 0.0))
        for j in range(NCORE - 1):
            P.w("gpsimd", sG)
            P.w("gpsimd", sV)
            P.o("gpsimd", lambda e, j=j: e.dma_start(
                out=scr(3, 0, width), in_=cc3[j]), inc=[(sG, 16)])
            P.w("vector", sG)
            VD(lambda e, half=half: e.tensor_tensor(
                out=scr(4, half, half), in0=scr(4, 0, half), in1=scr(3, 0, half),
                op=ALU.mult))
            VD(lambda e, half=half: e.tensor_tensor(
                out=scr(4, 0, half), in0=scr(4, half, half), in1=scr(3, half, half),
                op=ALU.add))
            VD(lambda e, j=j, half=half, pick_off=pick_off: e.scalar_tensor_tensor(
                out=sm(pick_off, half), in0=scr(4, 0, half),
                scalar=SELO[:, j + 1:j + 2], in1=sm(pick_off, half),
                op0=ALU.mult, op1=ALU.add))
            if j < 3 and width == 32:
                V(lambda e, j=j: e.tensor_copy(
                    out=SM[:, 2976 + j * 32: 2976 + (j + 1) * 32], in_=scr(3, 0, 32)))

    tap("exa0", lambda: sm(OF["exA0"], 32))
    exchange(OF["exA0"], 32, cc_l0_i, cc_l0_o, OF["carry"])
    VD(lambda e: e.tensor_tensor(out=sm(OF["gac"]), in0=sm(OF["carry"]),
                                 in1=sm(OF["ga0"]), op=ALU.mult))
    tap("carry0", lambda: sm(OF["carry"], 16))
    for c in range(DC):
        par = c % 2
        P.w("sync", sV)
        P.o("sync", lambda e, c=c, par=par: e.dma_start(
            out=scr(par), in_=xt[c * 128:(c + 1) * 128, :]), inc=[(sW, 16)])
        P.w("vector", sW)
        V(lambda e, c=c, par=par: e.scalar_tensor_tensor(
            out=scr(2), in0=h0b_fn(c), scalar=sm(OF["gac"])[:, c:c + 1],
            in1=scr(par), op0=ALU.mult, op1=ALU.add))
        V(lambda e, c=c: e.scalar_tensor_tensor(
            out=cell(F, c), in0=hloc_fn(c), scalar=sm(OF["ga0"])[:, c:c + 1],
            in1=scr(2), op0=ALU.mult, op1=ALU.add))
    tap("x1", lambda: F[:, 0: DC * TC])

    pbar("fix0")
    ln_phase(lambda c: (cell(F, c), None), lambda c: cell(Eb, c),
             lambda c: cell(Eb, DC + c), OF["cb10"], sm(OF["mc0"], 96)[:, 48:64])
    tap("h1b", lambda: Eb[:, DC * TC: 2 * DC * TC])
    for c in range(DC):
        V(lambda e, c=c: e.tensor_scalar(
            out=cell(F, c), in0=cell(F, c), scalar1=sm(OF["rgb0"])[:, c:c + 1],
            scalar2=None, op0=ALU.mult))

    pbar("ln2")
    P.w("tensor", sV)
    for t2 in range(2):
        P.wt("tensor", bank_free[6 + t2])
        for kc in range(DC):
            PE(lambda e, kc=kc, t2=t2: e.matmul(
                pb[6 + t2][0:8, :], GWTB[:, kc * 8:(kc + 1) * 8],
                cell(Eb, DC + kc, t2 * 512, 512), start=(kc == 0), stop=(kc == DC - 1)),
                inc=(kc == DC - 1))
        pe_t = sPE.t()
        P.wt("scalar", pe_t)
        Sc(lambda e, t2=t2: e.activation(out=SC8[:, t2 * 512:(t2 + 1) * 512],
                                         in_=pb[6 + t2][0:8, :], func=AF.Sigmoid))
    for tc8 in range(8):
        P.w("tensor", sS)
        P.w("tensor", sV)
        PE(lambda e, tc8=tc8: e.transpose(pb[6][:, 0:8],
                                          SC8[:, tc8 * 128:(tc8 + 1) * 128],
                                          IDENT[0:8, 0:8]))
        P.w("vector", sPE)
        r = R8
        VD(lambda e: e.tensor_tensor(out=r[:, 0:8], in0=pb[6][:, 0:8], in1=BIAST[:, :],
                                     op=ALU.add))
        VD(lambda e: e.reduce_max(r[:, 8:9], r[:, 0:8], axis=AX.X))
        VD(lambda e: e.tensor_scalar(out=r[:, 16:24], in0=r[:, 0:8], scalar1=r[:, 8:9],
                                     scalar2=None, op0=ALU.is_equal))
        VD(lambda e: e.scalar_tensor_tensor(out=r[:, 24:32], in0=r[:, 16:24],
                                            scalar=-1e30, in1=r[:, 0:8],
                                            op0=ALU.mult, op1=ALU.add))
        VD(lambda e: e.reduce_max(r[:, 9:10], r[:, 24:32], axis=AX.X))
        VD(lambda e: e.tensor_scalar(out=r[:, 32:40], in0=r[:, 24:32], scalar1=r[:, 9:10],
                                     scalar2=None, op0=ALU.is_equal))
        VD(lambda e: e.tensor_tensor(out=r[:, 16:24], in0=r[:, 16:24], in1=r[:, 32:40],
                                     op=ALU.add))
        VD(lambda e: e.tensor_tensor(out=r[:, 24:32], in0=pb[6][:, 0:8], in1=r[:, 16:24],
                                     op=ALU.mult))
        VD(lambda e: e.reduce_sum(r[:, 10:11], r[:, 24:32], axis=AX.X))
        VD(lambda e: e.reciprocal(out=r[:, 11:12], in_=r[:, 10:11]))
        VD(lambda e: e.tensor_scalar(out=r[:, 32:40], in0=r[:, 24:32], scalar1=r[:, 11:12],
                                     scalar2=None, op0=ALU.mult))
        P.w("tensor", sV)
        PE(lambda e, tc8=tc8: e.transpose(pb[7][0:8, 0:128], R8[:, 32:40], IDENT[:, :]))
        P.w("vector", sPE)
        V(lambda e, tc8=tc8: e.tensor_copy(out=WE8[:, tc8 * 128:(tc8 + 1) * 128],
                                           in_=pb[7][0:8, 0:128]))
    bank_free[6] = sV.t()
    bank_free[7] = sV.t()
    P.w("sync", sV)
    P.o("sync", lambda e: e.dma_start(out=we_dram, in_=WE8[:, :]), inc=[(sW, 16)])
    we_dram_t = sW.t()
    tap("we8", lambda: WE8[0:8, :])

    def up_pair_use(kind, e8=None):
        state = {}

        def use(j, tile, ready):
            pair, half = divmod(j, 2)
            parity = pair % 2
            banks = (0, 1, 2, 3) if parity == 0 else (4, 5, 6, 7)
            P.wt("tensor", ready)
            for t2 in range(2):
                bank = banks[half * 2 + t2]
                P.wt("tensor", bank_free[bank])
                for kc in range(DC):
                    PE(lambda e, kc=kc, bank=bank, t2=t2, tile=tile: e.matmul(
                        pb[bank][:, :], tile(kc), cell(Eb, DC + kc, t2 * 512, 512),
                        start=(kc == 0), stop=(kc == DC - 1)), inc=(kc == DC - 1))
            pe_t = sPE.t()
            if half == 0:
                state["first_banks"] = (banks[0], banks[1])
                return pe_t
            yb = state["first_banks"]
            gb_ = (banks[2], banks[3])
            if kind == "shared":
                sil_banks, mul_banks = gb_, yb
            else:
                sil_banks, mul_banks = yb, gb_
            P.wt("scalar", pe_t)
            P.w("scalar", sV)
            for t2 in range(2):
                Sc(lambda e, t2=t2, sb_=sil_banks: e.activation(
                    out=scr(1, t2 * 512, 512), in_=pb[sb_[t2]][:, :], func=AF.Silu))
            s_t = sS.t()
            P.wt("vector", s_t)
            for t2 in range(2):
                if kind == "exp":
                    V(lambda e, t2=t2, mb=mul_banks: e.tensor_tensor(
                        out=scr(2, t2 * 512, 512), in0=pb[mb[t2]][:, :],
                        in1=scr(0, t2 * 512, 512), op=ALU.mult))
                    V(lambda e, t2=t2, pair=pair: e.tensor_tensor(
                        out=cell(Eb, pair, t2 * 512, 512), in0=scr(1, t2 * 512, 512),
                        in1=scr(2, t2 * 512, 512), op=ALU.mult))
                else:
                    V(lambda e, t2=t2, mb=mul_banks, pair=pair: e.tensor_tensor(
                        out=cell(Eb, pair, t2 * 512, 512), in0=scr(1, t2 * 512, 512),
                        in1=pb[mb[t2]][:, :], op=ALU.mult))
            v_t = sV.t()
            for bk in (sil_banks + mul_banks):
                bank_free[bk] = v_t
            return pe_t
        return use

    def down_use(j, tile, ready):
        m = j % DC
        parity = m % 2
        P.wt("tensor", ready)
        for t2 in range(2):
            bank = parity * 2 + t2
            P.wt("tensor", bank_free[bank])
            for kc in range(HES):
                PE(lambda e, kc=kc, bank=bank, t2=t2, tile=tile: e.matmul(
                    pb[bank][:, :], tile(kc), cell(Eb, kc, t2 * 512, 512),
                    start=(kc == 0), stop=(kc == HES - 1)), inc=(kc == HES - 1))
            pe_t = sPE.t()
            P.wt("vector", pe_t)
            V(lambda e, m=m, t2=t2, bank=bank: e.tensor_tensor(
                out=cell(F, m, t2 * 512, 512), in0=cell(F, m, t2 * 512, 512),
                in1=pb[bank][:, :], op=ALU.add))
            bank_free[bank] = sV.t()
        return sPE.t()

    for kh in range(2):
        stream(sgw, 2 * HES, up_pair_use("shared"), first=kh * 2 * HES, kdim=D)
        stream(sdw, DC, down_use, first=kh * DC, kdim=1408)
    tap("om_sh", lambda: F[:, 0: DC * TC])
    for e8 in range(NEXP):
        P.wt("gpsimd", we_dram_t)
        P.w("gpsimd", sV)
        P.o("gpsimd", lambda e, e8=e8: e.dma_start(
            out=scr(0), in_=we_dram[e8:e8 + 1, :].partition_broadcast(128)),
            inc=[(sG, 16)])
        web_t = sG.t()
        P.wt("vector", web_t)
        stream(upw, 2 * HES, up_pair_use("exp", e8), first=e8 * 2 * HES, kdim=D)
        stream(dnw, DC, down_use, first=e8 * DC, kdim=1408)
    for c in range(DC):
        P.w("vector", sPE)
        V(lambda e, c=c: e.tensor_scalar(
            out=cell(F, c), in0=cell(F, c), scalar1=sm(OF["gb0"])[:, c:c + 1],
            scalar2=None, op0=ALU.mult))
    tap("x2", lambda: F[:, 0: DC * TC])

    pbar("ffn0")
    ln_phase(lambda c: (cell(F, c), None), lambda c: cell(Eb, c),
             lambda c: cell(Eb, DC + c), OF["ca11"], sm(OF["mc1"], 96)[:, 0:16])
    for c in range(DC):
        P.o("sync", lambda e, c=c: e.dma_start(out=x2_sp[c], in_=cell(F, c)),
            inc=[(sW, 16)])
    x2sp_t = sW.t()

    gstg_free = [None, None]

    def inpj_use(j, tile, ready):
        P.wt("tensor", ready)
        P.wt("vector", x2sp_t)
        is_gate = j >= DIC
        ms = j - DIC if is_gate else j
        for t2 in range(2):
            bank = (j % 2) * 2 + t2
            P.wt("tensor", bank_free[bank])
            for kc in range(DC):
                PE(lambda e, kc=kc, bank=bank, t2=t2, tile=tile: e.matmul(
                    pb[bank][:, :], tile(kc), cell(Eb, DC + kc, t2 * 512, 512),
                    start=(kc == 0), stop=(kc == DC - 1)), inc=(kc == DC - 1))
            pe_t = sPE.t()
            P.wt("vector", pe_t)
            if is_gate:
                par = ms % 2
                P.wt("vector", gstg_free[par])
                V(lambda e, t2=t2, bank=bank, par=par: e.tensor_copy(
                    out=scrb(par, t2 * 512, 512), in_=pb[bank][:, :]))
            else:
                V(lambda e, ms=ms, t2=t2, bank=bank: e.tensor_copy(
                    out=cell(Fb, ms, t2 * 512, 512), in_=pb[bank][:, :]))
            bank_free[bank] = sV.t()
        if is_gate:
            par = ms % 2
            P.w("sync", sV)
            P.o("sync", lambda e, ms=ms, par=par: e.dma_start(
                out=gate_sp[ms], in_=scrb(par, 0, TC)), inc=[(sW, 16)])
            gstg_free[par] = sW.t()
        return sPE.t()

    stream(inpj, 2 * DIC, inpj_use)
    tap("xmb", lambda: Fb[:, 0: DIC * TC])

    pbar("inpj")
    for c in range(DIC):
        V(lambda e, c=c: e.tensor_copy(out=sm(OF["agb"], 96)[:, c * 3:c * 3 + 3],
                                       in_=cell(Fb, c, 1021, 3)))
    P.w("gpsimd", sV)
    P.o("gpsimd", lambda e: e.dma_start(out=cc_h_i, in_=sm(OF["agb"], 96)),
        inc=[(sG, 16)])
    P.w("gpsimd", sG)
    P.o("gpsimd", lambda e: e.collective_compute(
        "AllGather", ALU.bypass, replica_groups=[list(range(NCORE))],
        ins=[cc_h_i], outs=[cc_h_o]), inc=[(sG, 1)])
    cch3 = cc_h_o.rearrange("(n p) f -> n p f", n=NCORE)
    P.w("vector", sG)
    V(lambda e: e.memset(sm(OF["agb"], 96), 0.0))
    for j in range(NCORE):
        P.w("gpsimd", sG)
        P.w("gpsimd", sV)
        P.o("gpsimd", lambda e, j=j: e.dma_start(out=scr(3, 0, 96), in_=cch3[j]),
            inc=[(sG, 16)])
        P.w("vector", sG)
        VD(lambda e, j=j: e.scalar_tensor_tensor(
            out=sm(OF["agb"], 96), in0=scr(3, 0, 96), scalar=SELP[:, j:j + 1],
            in1=sm(OF["agb"], 96), op0=ALU.mult, op1=ALU.add))
    P.wt("scalar", x2sp_t)
    P.w("scalar", sPE)
    for c in range(DIC):
        P.w("vector", sS)
        V(lambda e, c=c: e.tensor_copy(out=scrb(4, 0, 3),
                                       in_=sm(OF["agb"], 96)[:, c * 3:c * 3 + 3]))
        V(lambda e, c=c: e.tensor_copy(out=scrb(4, 3, TC), in_=cell(Fb, c)))
        V(lambda e, c=c: e.tensor_scalar(
            out=scr(0), in0=scrb(4, 0, TC), scalar1=CW[:, 4 * c:4 * c + 1],
            scalar2=None, op0=ALU.mult))
        for jj in range(1, 4):
            V(lambda e, c=c, jj=jj: e.scalar_tensor_tensor(
                out=scr(0), in0=scrb(4, jj, TC), scalar=CW[:, 4 * c + jj:4 * c + jj + 1],
                in1=scr(0), op0=ALU.mult, op1=ALU.add))
        P.w("scalar", sV)
        Sc(lambda e, c=c: e.activation(out=cell(Eb, c), in_=scr(0), func=AF.Silu,
                                       bias=CB[:, c:c + 1], scale=1.0))
    tap("xtb", lambda: Eb[:, 0: DIC * TC])

    pbar("conv")
    def xp_use(mwidth, banks):
        def use(j, tile, ready):
            kh = j
            P.wt("tensor", ready)
            P.w("tensor", sS)
            for t2 in range(2):
                bank = banks[t2]
                if kh == 0:
                    P.wt("tensor", bank_free[bank])
                for kc in range(DC):
                    PE(lambda e, kc=kc, kh=kh, bank=bank, t2=t2, tile=tile, mw=mwidth:
                       e.matmul(pb[bank][0:mw, :], tile(kc),
                                cell(Eb, kh * DC + kc, t2 * 512, 512),
                                start=(kh == 0 and kc == 0),
                                stop=(kh == 1 and kc == DC - 1)),
                       inc=(kc == DC - 1))
            return sPE.t()
        return use

    stream(xpja, 2, xp_use(128, (0, 1)), kdim=D, mw=128)
    stream(xpjb, 2, xp_use(32, (2, 3)), kdim=D, mw=32)
    pe_t = sPE.t()
    P.wt("vector", pe_t)
    for t2 in range(2):
        V(lambda e, t2=t2: e.tensor_copy(out=scrb(4, t2 * 512, 512), in_=pb[t2][:, :]))
        V(lambda e, t2=t2: e.tensor_copy(out=BCr[:, t2 * 512:(t2 + 1) * 512],
                                         in_=pb[2 + t2][0:32, :]))
    V(lambda e: e.tensor_scalar(out=BCr[0:16, :], in0=BCr[0:16, :], scalar1=-1.0,
                                scalar2=None, op0=ALU.mult))
    for bk in range(4):
        bank_free[bk] = sV.t()
    V(lambda e: e.tensor_copy(out=BCb[:, :], in_=BCr[:, :]))
    P.w("sync", sV)
    P.o("sync", lambda e: e.dma_start(out=bc_dram, in_=BCr[:, :]), inc=[(sW, 16)])
    P.o("sync", lambda e: e.dma_start(out=bcb_dram, in_=BCb[:, :]), inc=[(sW, 16)])
    bc_dram_t = sW.t()
    tap("bcr", lambda: BCr[0:32, :])
    tap("tsb", lambda: scrb(4, 0, TC))

    def dt_use(j, tile, ready):
        m = j
        P.wt("tensor", ready)
        for t2 in range(2):
            bank = (m % 2) * 2 + t2
            P.wt("tensor", bank_free[bank])
            PE(lambda e, bank=bank, t2=t2, tile=tile: e.matmul(
                pb[bank][:, :], tile(0), scrb(4, t2 * 512, 512),
                start=True, stop=True), inc=True)
            pe_t = sPE.t()
            P.wt("scalar", pe_t)
            Sc(lambda e, m=m, t2=t2, bank=bank: e.activation(
                out=scr(m % 4, t2 * 512, 512), in_=pb[bank][:, :], func=AF.Sigmoid,
                bias=sm(OF["ndtb"], 32)[:, m:m + 1], scale=-1.0))
            bank_free[bank] = sS.t()
        if m % 4 == 3:
            for mb in range(m - 3, m + 1):
                Sc(lambda e, mb=mb: e.activation(out=cell(Fb, mb), in_=scr(mb % 4),
                                                 func=AF.Ln))
        return sPE.t()

    stream(dtpj, DIC, dt_use, kdim=128)
    tap("dtn", lambda: Fb[:, 0: DIC * TC])
    P.w("vector", sS)
    for c in range(DIC):
        par = c % 2
        P.w("vector", sW)
        V(lambda e, c=c, par=par: e.tensor_scalar(
            out=scr(par), in0=cell(Eb, c), scalar1=DPT[:, c:c + 1], scalar2=None,
            op0=ALU.mult))
        P.w("sync", sV)
        P.o("sync", lambda e, c=c, par=par: e.dma_start(out=y_sp[c], in_=scr(par)),
            inc=[(sW, 16)])
        V(lambda e, c=c: e.tensor_tensor(out=cell(Eb, c), in0=cell(Fb, c),
                                         in1=cell(Eb, c), op=ALU.mult))
        V(lambda e, c=c: e.reduce_sum(sm(OF["sdt"], 32)[:, c:c + 1], cell(Fb, c),
                                      axis=AX.X))
        P.w("scalar", sV)
        Sc(lambda e, c=c: e.activation(
            out=sm(OF["exAm"], 512)[:, c * 16:(c + 1) * 16],
            in_=sm(OF["negA"], 512)[:, c * 16:(c + 1) * 16], func=AF.Exp,
            scale=sm(OF["sdt"], 32)[:, c:c + 1]))
    tap("ub", lambda: Eb[:, 0: DIC * TC])

    pbar("uprep")
    a_free = [None, None]
    bc_free = [None, None]
    g_free = [None, None]
    it_n = [0]
    for g in range(DIC // 4):
        cs = list(range(g * 4, g * 4 + 4))
        for c in cs:
            P.w("sync", sV)
            P.w("sync", sG)
            P.w("sync", sW)
            P.o("sync", lambda e, c=c: e.dma_start(out=cell(YG, c % 4),
                                                   in_=y_sp[c]), inc=[(sW, 16)])
        yld_t = sW.t()
        P.wt("gpsimd", yld_t)
        for s in range(NST):
            spar = s % 2
            P.wt("gpsimd", bc_dram_t)
            P.wt("gpsimd", bc_free[spar])
            P.o("gpsimd", lambda e, s=s, spar=spar: e.dma_start(
                out=scrb(3 + spar, 0, TC),
                in_=bcb_dram[s:s + 1, :].partition_broadcast(128)), inc=[(sG, 16)])
            P.o("gpsimd", lambda e, s=s, spar=spar: e.dma_start(
                out=scrb(3 + spar, TC, TC),
                in_=bcb_dram[16 + s:17 + s, :].partition_broadcast(128)), inc=[(sG, 16)])
            bct = sG.t()
            for ci, c in enumerate(cs):
                P.wt("scalar", a_free[0])
                Sc(lambda e, c=c, s=s: e.activation(
                    out=scr(0), in_=cell(Fb, c), func=AF.Exp,
                    scale=sm(OF["negA"], 512)[:, c * 16 + s:c * 16 + s + 1]))
                at = sS.t()
                P.wt("vector", bct)
                P.wt("vector", yld_t)
                V(lambda e, c=c, spar=spar: e.tensor_tensor(
                    out=scrb(1, 0, TC), in0=cell(Eb, c),
                    in1=scrb(3 + spar, 0, TC), op=ALU.mult))
                P.wt("vector", at)
                V(lambda e: e.tensor_tensor_scan(
                    out=scrb(1, TC, TC), data0=scr(0), data1=scrb(1, 0, TC),
                    initial=cZ, op0=ALU.mult, op1=ALU.add))
                tpar = it_n[0] % 2
                it_n[0] += 1
                P.wt("vector", g_free[tpar])
                V(lambda e, spar=spar, tpar=tpar: e.tensor_tensor(
                    out=scrb(2, tpar * TC, TC), in0=scrb(1, TC, TC),
                    in1=scrb(3 + spar, TC, TC), op=ALU.mult))
                V(lambda e, c=c, s=s: e.tensor_copy(
                    out=sm(OF["exHm"], 512)[:, c * 16 + s:c * 16 + s + 1],
                    in_=scrb(1, TC + 1023, 1)))
                vt = sV.t()
                P.wt("gpsimd", vt)
                P.o("gpsimd", lambda e, c=c, tpar=tpar: e.tensor_tensor(
                    out=cell(YG, c % 4), in0=cell(YG, c % 4), in1=scrb(2, tpar * TC, TC),
                    op=ALU.add), inc=[(sG, 1)])
                g_free[tpar] = sG.t()
                a_free[0] = sV.t()
            bc_free[spar] = sV.t()
        for c in cs:
            P.w("sync", sV)
            P.w("sync", sG)
            P.o("sync", lambda e, c=c: e.dma_start(out=y_sp[c], in_=cell(YG, c % 4)),
                inc=[(sW, 16)])
    ysp_t = sW.t()
    tap("exhm", lambda: sm(OF["exHm"], 512))

    pbar("scanm")
    exchange(OF["exAm"], 1024, cc_m_i, cc_m_o, OF["carrym"])
    tap("carrym", lambda: sm(OF["carrym"], 512))

    pbar("e3")
    for s in range(NST):
        P.wt("gpsimd", bc_dram_t)
        P.w("gpsimd", sV)
        P.o("gpsimd", lambda e, s=s: e.dma_start(
            out=scr(3), in_=bc_dram[16 + s:17 + s, :].partition_broadcast(128)),
            inc=[(sG, 16)])
        P.w("vector", sG)
        V(lambda e, s=s: e.tensor_copy(out=cell(Eb, s), in_=scr(3)))
    e_free = [None, None]
    for c in range(DIC):
        P.w("vector", sS)
        V(lambda e, c=c: e.tensor_tensor_scan(
            out=scr(0), data0=cell(Fb, c), data1=cell(Fb, c), initial=cZ,
            op0=ALU.add, op1=ALU.bypass))
        V(lambda e: e.memset(scr(1), 0.0))
        cdt_t = sV.t()
        for s in range(NST):
            epar = s % 2
            P.wt("scalar", e_free[epar])
            P.wt("scalar", cdt_t)
            Sc(lambda e, c=c, s=s, epar=epar: e.activation(
                out=cell(E, 8 + epar), in_=scr(0), func=AF.Exp,
                scale=sm(OF["negA"], 512)[:, c * 16 + s:c * 16 + s + 1]))
            et = sS.t()
            P.wt("vector", et)
            V(lambda e, c=c, s=s, epar=epar: e.scalar_tensor_tensor(
                out=scrb(4, 0, TC), in0=cell(E, 8 + epar),
                scalar=sm(OF["carrym"], 512)[:, c * 16 + s:c * 16 + s + 1],
                in1=cell(Eb, s), op0=ALU.mult, op1=ALU.mult))
            V(lambda e: e.tensor_tensor(out=scr(1), in0=scr(1), in1=scrb(4, 0, TC),
                                        op=ALU.add))
            e_free[epar] = sV.t()
        P.w("sync", sV)
        P.w("sync", sS)
        P.wt("sync", ysp_t)
        P.o("sync", lambda e, c=c: e.dma_start(out=scr(2), in_=y_sp[c]), inc=[(sW, 16)])
        P.o("sync", lambda e, c=c: e.dma_start(out=scrb(4, TC, TC), in_=gate_sp[c]),
            inc=[(sW, 16)])
        P.w("vector", sW)
        V(lambda e: e.tensor_tensor(out=scr(2), in0=scr(2), in1=scr(1), op=ALU.add))
        P.w("scalar", sW)
        P.w("scalar", sV)
        Sc(lambda e: e.activation(out=scr(3), in_=scrb(4, TC, TC), func=AF.Silu))
        P.w("vector", sS)
        V(lambda e, c=c: e.tensor_tensor(out=cell(Fb, c), in0=scr(2), in1=scr(3),
                                         op=ALU.mult))
    tap("y2b", lambda: Fb[:, 0: DIC * TC])

    pbar("fixm")
    def op_use(j, tile, ready):
        m, kh = divmod(j, 2)
        P.wt("tensor", ready)
        if kh == 0:
            par = m % 2
            P.w("sync", sV)
            P.w("sync", sS)
            P.wt("sync", x2sp_t)
            P.o("sync", lambda e, m=m, par=par: e.dma_start(
                out=scr(par), in_=x2_sp[m]), inc=[(sW, 16)])
        for t2 in range(2):
            bank = (m % 2) * 2 + t2
            if kh == 0:
                P.wt("tensor", bank_free[bank])
            for kc in range(DC):
                PE(lambda e, kc=kc, kh=kh, bank=bank, t2=t2, tile=tile: e.matmul(
                    pb[bank][:, :], tile(kc), cell(Fb, kh * DC + kc, t2 * 512, 512),
                    start=(kh == 0 and kc == 0), stop=(kh == 1 and kc == DC - 1)),
                    inc=(kc == DC - 1))
        if kh == 1:
            pe_t = sPE.t()
            P.wt("vector", pe_t)
            P.w("vector", sW)
            par = m % 2
            for t2 in range(2):
                bank = (m % 2) * 2 + t2
                V(lambda e, m=m, t2=t2, bank=bank, par=par: e.scalar_tensor_tensor(
                    out=cell(E, m, t2 * 512, 512), in0=pb[bank][:, :],
                    scalar=sm(OF["ga1"])[:, m:m + 1],
                    in1=scr(par, t2 * 512, 512), op0=ALU.mult, op1=ALU.add))
                bank_free[bank] = sV.t()
        return sPE.t()

    barrier()
    stream(outpj, 2 * DC, op_use)
    barrier()
    tap("x3", lambda: E[:, 0: DC * TC])

    pbar("outpj")
    ln_phase(lambda c: (cell(E, c), None), lambda c: cell(Fb, c),
             lambda c: cell(Fb, DC + c), OF["cb11"], sm(OF["mc1"], 96)[:, 48:64],
             rows_tap="rows4")
    tap("h3b", lambda: Fb[:, DC * TC: 2 * DC * TC])
    tap("xb4", lambda: Fb[:, 0: DC * TC])

    def tern_prep(j, par, load_t):
        P.wt("vector", load_t)
        P.wt("vector", ws["stgb_free"][par])
        V(lambda e, par=par: e.tensor_scalar(
            out=STGB[par][:, :], in0=STGF[:, :], scalar1=thP, scalar2=None,
            op0=ALU.is_gt))
        V(lambda e, par=par: e.tensor_scalar(
            out=scrb(2, 0, 2048), in0=STGF[:, :], scalar1=nthP, scalar2=None,
            op0=ALU.is_lt))
        V(lambda e, par=par: e.tensor_tensor(
            out=STGB[par][:, :], in0=STGB[par][:, :], in1=scrb(2, 0, 2048),
            op=ALU.subtract))
        return sV.t(), STGB[par]

    def fw2_use(j, tile, ready):
        m = j
        par = m % 2
        P.wt("tensor", ready)
        for t2 in range(2):
            bank = (m % 2) * 2 + t2
            P.wt("tensor", bank_free[bank])
            for kc in range(DC):
                PE(lambda e, kc=kc, bank=bank, t2=t2, tile=tile: e.matmul(
                    pb[bank][:, :], tile(kc), cell(Fb, DC + kc, t2 * 512, 512),
                    start=(kc == 0), stop=(kc == DC - 1)), inc=(kc == DC - 1))
            pe_t = sPE.t()
            P.wt("vector", pe_t)
            P.w("vector", sW)
            V(lambda e, m=m, t2=t2, bank=bank: e.tensor_scalar(
                out=scr(1, t2 * 512, 512), in0=pb[bank][:, :], scalar1=scP,
                scalar2=FBP[:, m:m + 1], op0=ALU.mult, op1=ALU.add))
            V(lambda e, m=m, t2=t2, par=par: e.scalar_tensor_tensor(
                out=scr(3 + (0 if par == 0 else 1), t2 * 512, 512),
                in0=scr(1, t2 * 512, 512),
                scalar=sm(OF["gb1"])[:, m:m + 1], in1=cell(E, m, t2 * 512, 512),
                op0=ALU.mult, op1=ALU.add))
            bank_free[bank] = sV.t()
        P.w("sync", sV)
        P.o("sync", lambda e, m=m, par=par: e.dma_start(
            out=yo[m * 128:(m + 1) * 128, :], in_=scr(3 + (0 if par == 0 else 1))),
            inc=[(sO, 16)])
        return sPE.t()

    stream(fw, DC, fw2_use, prep_fn=tern_prep)

    P.w("gpsimd", sO, sO.n)
    P.w("gpsimd", sW)

    with nc.Block() as block:
        for eng in _Prog.ENGS:
            def make(eng):
                def body(e):
                    for fn in P.streams[eng]:
                        fn(e)
                return body
            getattr(block, eng)(make(eng))
    es.close()
    return nc, tapn


def _host_layout(inp):
    f = np.float32
    asc = np.ascontiguousarray

    def strips(wT, mw=128):
        K, M = wT.shape
        a = wT.reshape(K // 128, 128, M // mw, mw)
        return asc(a.transpose(2, 1, 0, 3).reshape(M // mw, 128, (K // 128) * mw))

    def colP(v):
        n = v.shape[0]
        return asc(v.reshape(n // 128, 128).T)

    x = np.asarray(inp["L0_attn_x"], f)[0]
    win_T = np.asarray(inp["L0_attn_W_in"], f).T
    wg_T = np.asarray(inp["L0_attn_W_gate"], f).T
    sgw_T = np.asarray(inp["L0_ffn_shared_gw"], f).T
    ys = strips(sgw_T[:, :2816])
    gs = strips(sgw_T[:, 2816:])
    sgw_s = np.empty((44, 128, 2048), f)
    sgw_s[0::2] = ys
    sgw_s[1::2] = gs
    sdw_T = np.asarray(inp["L0_ffn_shared_dw"], f).T
    sdw_s = np.empty((32, 128, 1408), f)
    for kh in range(2):
        sdw_s[kh * 16:(kh + 1) * 16] = strips(sdw_T[kh * 1408:(kh + 1) * 1408, :])
    upwf = np.asarray(inp["L0_ffn_up_w"], f)
    upw_s = np.empty((8 * 22, 128, 2048), f)
    for e in range(8):
        uT = upwf[e].T
        gse = strips(uT[:, :1408])
        use_ = strips(uT[:, 1408:])
        upw_s[e * 22 + 0: e * 22 + 22: 2] = gse
        upw_s[e * 22 + 1: e * 22 + 22: 2] = use_
    dnwf = np.asarray(inp["L0_ffn_down_w"], f)
    dnw_s = np.empty((8 * 16, 128, 1408), f)
    for e in range(8):
        dnw_s[e * 16:(e + 1) * 16] = strips(dnwf[e].T)
    inpj_s = strips(np.asarray(inp["L1_attn_in_proj_w"], f).T)
    xpjT = np.asarray(inp["L1_attn_x_proj_w"], f).T
    xpja = np.concatenate([strips(xpjT[:2048, :128]), strips(xpjT[2048:, :128])])
    xpjb = np.concatenate([strips(xpjT[:2048, 128:], mw=32), strips(xpjT[2048:, 128:], mw=32)])
    dtpj_s = strips(np.asarray(inp["L1_attn_dt_proj_w"], f).T)
    opjT = np.asarray(inp["L1_attn_out_proj_w"], f).T
    opj_s = np.empty((32, 128, 2048), f)
    for kh in range(2):
        st_ = strips(opjT[kh * 2048:(kh + 1) * 2048, :])
        for m in range(16):
            opj_s[m * 2 + kh] = st_[m]
    fw_s = strips(np.asarray(inp["L1_ffn_weight"], f).T)
    gwT = np.asarray(inp["L0_ffn_gate_w"], f).T
    gw_t = asc(gwT.reshape(16, 128, 8).transpose(1, 0, 2).reshape(128, 128))
    Af = np.asarray(inp["L1_attn_A"], f)
    AT = asc(Af.reshape(32, 128, 16).transpose(1, 0, 2).reshape(128, 512))
    cwf = np.asarray(inp["L1_attn_conv_w"], f)
    cwT = asc(cwf.reshape(32, 128, 4).transpose(1, 0, 2).reshape(128, 128))
    consts = np.zeros((128, 8), f)
    consts[:, 1] = 1.0
    consts[:, 2] = 1.000001
    consts[:, 3] = 1e-5
    common = dict(
        win=strips(win_T), wgate=strips(wg_T), sgw=sgw_s, sdw=sdw_s,
        upw=upw_s, dnw=dnw_s, inpj=inpj_s, xpja=xpja, xpjb=xpjb,
        dtpj=dtpj_s, outpj=opj_s, fw=fw_s, gw=gw_t,
        cv0=colP(np.asarray(inp["L0_c"], f)[0]),
        cv1=colP(np.asarray(inp["L1_c"], f)[0]),
        flP=colP(np.asarray(inp["L0_attn_forget_lambda"], f)),
        binP=colP(np.asarray(inp["L0_attn_b_in"], f)),
        bgP=colP(np.asarray(inp["L0_attn_b_gate"], f)),
        cwP=cwT, cbP=colP(np.asarray(inp["L1_attn_conv_b"], f)),
        dtbP=colP(np.asarray(inp["L1_attn_dt_proj_b"], f)),
        AT=AT, DPi=colP(np.asarray(inp["L1_attn_D"], f)),
        fbP=colP(np.asarray(inp["L1_ffn_bias"], f)),
        biasT=np.tile(np.asarray(inp["L0_ffn_gate_bias"], f)[None, :], (128, 1)),
        ident=np.eye(128, dtype=f),
        onesr=np.ones((1, 128), f),
        onescb=np.ones((128, 1), ml_dtypes.bfloat16),
        onescf=np.ones((128, 1), f),
        consts=consts,
    )
    wmodf = np.asarray(inp["L0_wmod"], f)
    wmod1f = np.asarray(inp["L1_wmod"], f)
    maps = []
    for i in range(NCORE):
        m = dict(common)
        m["xt"] = asc(x[i * TC:(i + 1) * TC, :].T)
        wm = np.empty((24, 128, 2048), f)
        wm[:12] = strips(wmodf[:, i * 1536:(i + 1) * 1536])
        wm[12:] = strips(wmod1f[:, i * 1536:(i + 1) * 1536])
        m["wmod"] = wm
        so = np.zeros((128, NCORE), f)
        so[:, i] = 1.0
        sp = np.zeros((128, NCORE), f)
        if i > 0:
            sp[:, i - 1] = 1.0
        m["selown"] = so
        m["selprev"] = sp
        maps.append(m)
    return maps


def kernel(**inputs):
    maps = _host_layout(inputs)
    nc, tapn = _build()
    res = run_bass_kernel_spmd(nc, maps, core_ids=list(range(NCORE)))
    outs = [r["yo"] for r in res.results]
    y = np.concatenate([o.T for o in outs], axis=0).reshape(1, T, D)
    if _DEBUG_TAPS:
        kernel.taps = [{k: v for k, v in r.items() if k.startswith("tap_")}
                       for r in res.results]
    return np.ascontiguousarray(y.astype(np.float32))


# version 07 (final)
# speedup vs baseline: 1.0688x; 1.0688x over previous
import contextlib
import numpy as np
import ml_dtypes
import concourse.bass as bass
import concourse.mybir as mybir
from concourse.bass_utils import run_bass_kernel_spmd

F32 = mybir.dt.float32
BF16 = mybir.dt.bfloat16
AF = mybir.ActivationFunctionType
ALU = mybir.AluOpType
AX = mybir.AxisListType

NCORE = 8
T = 8192
TC = T // NCORE
D = 2048
DC = D // 128
DI = 4096
DIC = DI // 128
NST = 16
NEXP = 8
HES = 11
MSHC = 12

_DEBUG_TAPS = set()
_PHASE_BARRIERS = {"all"}


class _Sem:
    def __init__(self, es, nc, name):
        self.h = es.enter_context(nc.semaphore(name))
        self.n = 0
        self.name = name

    def t(self):
        return (self, self.n)


class _Prog:
    ENGS = ("sync", "vector", "scalar", "tensor", "gpsimd")

    def __init__(self):
        self.streams = {e: [] for e in self.ENGS}
        self.waited = {e: {} for e in self.ENGS}

    def o(self, eng, fn, inc=None):
        if inc:
            def wrapped(e, fn=fn, inc=tuple(inc)):
                ins = fn(e)
                for sem, k in inc:
                    ins = ins.then_inc(sem.h, k)
                return ins
            self.streams[eng].append(wrapped)
            for sem, k in inc:
                sem.n += k
        else:
            self.streams[eng].append(fn)

    def w(self, eng, sem, val=None):
        if val is None:
            val = sem.n
        if val <= 0 or self.waited[eng].get(sem.name, -1) >= val:
            return
        self.waited[eng][sem.name] = val
        self.streams[eng].append(lambda e, s=sem.h, v=val: e.wait_ge(s, v))

    def wt(self, eng, ticket):
        if ticket is not None:
            self.w(eng, ticket[0], ticket[1])


def _build():
    nc = bass.Bass("TRN2", target_bir_lowering=False)

    def din(name, shape, dt=F32):
        return nc.dram_tensor(name, list(shape), dt, kind="ExternalInput").ap()

    xt = din("xt", [D, TC])
    wmod = din("wmod", [2 * MSHC, 128, D])
    win = din("win", [DC, 128, D])
    wgate = din("wgate", [DC, 128, D])
    sgw = din("sgw", [4 * HES, 128, D])
    sdw = din("sdw", [2 * DC, 128, 1408])
    upw = din("upw", [NEXP * 2 * HES, 128, D])
    dnw = din("dnw", [NEXP * DC, 128, 1408])
    inpj = din("inpj", [2 * DIC, 128, D])
    xpja = din("xpja", [2, 128, D])
    xpjb = din("xpjb", [2, 128, 512])
    dtpj = din("dtpj", [DIC, 128, 128])
    outpj = din("outpj", [2 * DC, 128, D])
    fw = din("fw", [DC, 128, D])
    gw = din("gw", [128, DC * NEXP])
    cv0 = din("cv0", [128, DC])
    cv1 = din("cv1", [128, DC])
    flP = din("flP", [128, DC])
    binP = din("binP", [128, DC])
    bgP = din("bgP", [128, DC])
    cwP = din("cwP", [128, DIC * 4])
    cbP = din("cbP", [128, DIC])
    dtbP = din("dtbP", [128, DIC])
    AT = din("AT", [128, 512])
    DPi = din("DPi", [128, DIC])
    fbP = din("fbP", [128, DC])
    biasT = din("biasT", [128, NEXP])
    ident = din("ident", [128, 128])
    onesr = din("onesr", [1, 128])
    onescb = din("onescb", [128, 1], BF16)
    onescf = din("onescf", [128, 1])
    consts = din("consts", [128, 8])
    selown = din("selown", [128, NCORE])
    selprev = din("selprev", [128, NCORE])

    yo = nc.dram_tensor("yo", [D, TC], F32, kind="ExternalOutput").ap()

    def dint(name, shape, dt=F32):
        return nc.dram_tensor(name, list(shape), dt, kind="Internal").ap()

    cc_mod_i = dint("cc_mod_i", [128, 2 * MSHC])
    cc_mod_o = dint("cc_mod_o", [128 * NCORE, 2 * MSHC])
    cc_l0_i = dint("cc_l0_i", [128, 2 * DC])
    cc_l0_o = dint("cc_l0_o", [128 * NCORE, 2 * DC])
    cc_h_i = dint("cc_h_i", [128, 3 * DIC])
    cc_h_o = dint("cc_h_o", [128 * NCORE, 3 * DIC])
    cc_m_i = dint("cc_m_i", [128, 2 * DIC * NST])
    cc_m_o = dint("cc_m_o", [128 * NCORE, 2 * DIC * NST])
    gate_sp = dint("gate_sp", [DIC, 128, TC], BF16)
    bc_dram = dint("bc_dram", [32, TC])
    bcb_dram = dint("bcb_dram", [32, TC], BF16)
    we_dram = dint("we_dram", [NEXP, TC])
    x2_sp = dint("x2_sp", [DC, 128, TC])
    y_sp = dint("y_sp", [DIC, 128, TC])

    es = contextlib.ExitStack()

    def sbuf(name, p, f, dt=F32):
        return es.enter_context(nc.sbuf_tensor(name, [p, f], dt))

    E = sbuf("arE", 128, 16384)
    F = sbuf("arF", 128, 16384)
    Eb, Fb = E.bitcast(BF16), F.bitcast(BF16)
    STGF = sbuf("stgf", 128, 2048)
    STGB = [sbuf("stgb0", 128, 2048, BF16), sbuf("stgb1", 128, 2048, BF16)]
    SCR = sbuf("scr", 128, 5 * 1024)
    SCRb = SCR.bitcast(BF16)
    ROWS = SCR[0:1, 0:3072]
    YG = sbuf("yg", 128, 4096)
    BCr = sbuf("bcr", 32, TC)
    BCb = sbuf("bcb", 32, TC, BF16)
    R8 = sbuf("r8", 128, 48)
    SM = sbuf("sm", 128, 3072)
    SMC = sbuf("smc", 128, 8)
    IDENT = sbuf("identS", 128, 128)
    ONESR = sbuf("onesrS", 1, 128)
    ONECB = sbuf("onecbS", 128, 1, BF16)
    ONECF = sbuf("onecfS", 128, 1)
    GWT = sbuf("gwtS", 128, DC * NEXP)
    GWTB = sbuf("gwtbS", 128, DC * NEXP, BF16)
    BIAST = sbuf("biastS", 128, NEXP)
    SELO = sbuf("seloS", 128, NCORE)
    SELP = sbuf("selpS", 128, NCORE)
    BINP = sbuf("binpS", 128, DC)
    BGP = sbuf("bgpS", 128, DC)
    CW = sbuf("cwS", 128, DIC * 4)
    CB = sbuf("cbS", 128, DIC)
    DTB = sbuf("dtbS", 128, DIC)
    DPT = sbuf("dpS", 128, DIC)
    FBP = sbuf("fbpS", 128, DC)
    modsh = sbuf("modsh", 128, 2 * MSHC)
    SC8 = E[0:8, 5632:6656]
    WE8 = E[0:8, 6656:7680]

    OF = dict(scT0=0, scT1=16, mc0=32, mc1=128, ca10=224, ga0=240, cb10=256,
              gb0=272, ca11=288, ga1=304, cb11=320, gb1=336, sfl=352,
              gac=368, negA=384, ndtb=896, sdt=928, tA=960, tB=976,
              exA0=1024, exH0=1040, carry=1056, rgb0=1072, rgb1=1088,
              th=1104, comb=1120, pick=1136,
              exAm=1440, exHm=1952, carrym=2464)
    OF["agb"] = OF["exAm"]

    def sm(off, w=16):
        return SM[:, off: off + w]

    cZ, cONE, cEPS6, cEPS5 = SMC[:, 0:1], SMC[:, 1:2], SMC[:, 2:3], SMC[:, 3:4]

    sW = _Sem(es, nc, "sW")
    sPE = _Sem(es, nc, "sPE")
    sV = _Sem(es, nc, "sV")
    sS = _Sem(es, nc, "sS")
    sG = _Sem(es, nc, "sG")
    sO = _Sem(es, nc, "sO")
    ALLSEMS = [sW, sPE, sV, sS, sG, sO]
    P = _Prog()

    pb = [es.enter_context(nc.psum_tensor(f"pb{i}", [128, 512], F32)) for i in range(8)]

    def cell(buf, c, t0=0, wd=TC):
        return buf[:, c * TC + t0: c * TC + t0 + wd]

    def scr(slot, t0=0, wd=TC):
        return SCR[:, slot * TC + t0: slot * TC + t0 + wd]

    def scrb(slot, t0=0, wd=TC):
        return SCRb[:, 2 * slot * TC + t0: 2 * slot * TC + t0 + wd]

    def barrier():
        for e in _Prog.ENGS:
            for s in ALLSEMS:
                P.w(e, s)

    def pbar(name):
        if "all" in _PHASE_BARRIERS or name in _PHASE_BARRIERS:
            barrier()

    tapn = []

    def tap(name, ap_fn):
        if name not in _DEBUG_TAPS:
            return
        ap = ap_fn()
        if ap.dtype == BF16:
            ap = ap.bitcast(F32)
        o = nc.dram_tensor("tap_" + name, list(ap.shape), ap.dtype,
                           kind="ExternalOutput").ap()
        tapn.append("tap_" + name)
        barrier()
        nfree = ap.shape[-1]
        step = max(1, min(nfree, (2 * 1024 * 1024) // (128 * 4)))
        for f0 in range(0, nfree, step):
            f1 = min(nfree, f0 + step)
            P.o("gpsimd", lambda e, o=o, ap=ap, f0=f0, f1=f1: e.dma_start(
                out=o[:, f0:f1], in_=ap[:, f0:f1]), inc=[(sO, 16)])
        barrier()

    def V(fn, inc=True):
        P.o("vector", fn, inc=[(sV, 1)] if inc else None)

    def Sc(fn, inc=True):
        P.o("scalar", fn, inc=[(sS, 1)] if inc else None)

    def PE(fn, inc=True):
        P.o("tensor", fn, inc=[(sPE, 1)] if inc else None)

    def VD(fn):
        V(fn)
        P.w("vector", sV)

    ws = dict(idx=0, stg_free=None, stgb_free=[None, None])

    def stream(dram3, nstrips, use_fn, first=0, cast=True, mw=128, kdim=D,
               prep_fn=None):
        kc_n = kdim // 128
        for j in range(nstrips):
            par = ws["idx"] % 2
            ws["idx"] += 1
            P.wt("sync", ws["stg_free"])
            src = dram3[first + j]
            dst = STGF[:, : kc_n * mw]
            P.o("sync", lambda e, s=src, d=dst: e.dma_start(out=d, in_=s), inc=[(sW, 16)])
            load_t = sW.t()
            if prep_fn is not None:
                ready, buf = prep_fn(j, par, load_t)
                ws["stg_free"] = ready
            elif cast:
                P.wt("vector", load_t)
                P.wt("vector", ws["stgb_free"][par])
                V(lambda e, par=par, n=kc_n * mw: e.tensor_copy(
                    out=STGB[par][:, :n], in_=STGF[:, :n]))
                ready = sV.t()
                ws["stg_free"] = ready
                buf = STGB[par]
            else:
                ready = load_t
                buf = STGF

            def tile(kc, buf=buf, mw=mw):
                return buf[:, kc * mw:(kc + 1) * mw]
            pe_t = use_fn(j, tile, ready)
            if prep_fn is not None or cast:
                ws["stgb_free"][par] = pe_t
            else:
                ws["stg_free"] = pe_t

    bank_free = [None] * 8

    tiny_pairs = [
        (SMC[:, :], consts), (sm(OF["tA"]), cv0), (sm(OF["tB"]), cv1),
        (sm(OF["sfl"]), flP), (IDENT[:, :], ident), (ONESR[:, :], onesr),
        (ONECB[:, :], onescb), (ONECF[:, :], onescf), (GWT[:, :], gw),
        (BIAST[:, :], biasT), (SELO[:, :], selown), (SELP[:, :], selprev),
        (BINP[:, :], binP), (BGP[:, :], bgP), (CW[:, :], cwP), (CB[:, :], cbP),
        (DTB[:, :], dtbP), (DPT[:, :], DPi), (FBP[:, :], fbP),
        (SCR[:, 0:512], AT),
    ]
    for dst, src in tiny_pairs:
        P.o("sync", lambda e, d=dst, s=src: e.dma_start(out=d, in_=s), inc=[(sW, 16)])

    P.w("vector", sW)
    V(lambda e: e.tensor_copy(out=GWTB[:, :], in_=GWT[:, :]))
    V(lambda e: e.tensor_scalar(out=sm(OF["negA"], 512), in0=SCR[:, 0:512],
                                scalar1=-1.0, scalar2=None, op0=ALU.mult))
    V(lambda e: e.tensor_scalar(out=sm(OF["ndtb"], 32), in0=DTB[:, 0:32],
                                scalar1=-1.0, scalar2=None, op0=ALU.mult))
    P.w("scalar", sW)
    Sc(lambda e: e.activation(out=sm(OF["scT0"]), in_=sm(OF["tA"]), func=AF.Silu))
    Sc(lambda e: e.activation(out=sm(OF["scT1"]), in_=sm(OF["tB"]), func=AF.Silu))
    DBG = SM[:, 2976:3072]
    V(lambda e: e.tensor_scalar(out=DBG[:, 0:16], in0=sm(OF["sfl"]),
                                scalar1=17.0, scalar2=None, op0=ALU.min))
    P.w("scalar", sV)
    Sc(lambda e: e.activation(out=DBG[:, 16:32], in_=DBG[:, 0:16], func=AF.Exp))
    Sc(lambda e: e.activation(out=DBG[:, 32:48], in_=DBG[:, 16:32], func=AF.Ln,
                              bias=cONE, scale=1.0))
    P.w("vector", sS)
    VD(lambda e: e.tensor_tensor(out=DBG[:, 48:64], in0=DBG[:, 32:48],
                                 in1=sm(OF["sfl"]), op=ALU.max))
    VD(lambda e: e.tensor_scalar(out=sm(OF["sfl"]), in0=DBG[:, 48:64],
                                 scalar1=-8.0, scalar2=None, op0=ALU.mult))

    V(lambda e: e.memset(sm(OF["tA"], 1), 0.0))

    def fw1_use(j, tile, ready):
        P.wt("vector", ready)
        VD(lambda e: e.tensor_reduce(
            sm(OF["tB"], 1), STGF[:, 0:2048], axis=AX.X, op=ALU.add,
            apply_absolute_value=True))
        VD(lambda e: e.tensor_tensor(out=sm(OF["tA"], 1), in0=sm(OF["tA"], 1),
                                     in1=sm(OF["tB"], 1), op=ALU.add))
        return sV.t()

    stream(fw, DC, fw1_use, cast=False)
    P.w("tensor", sV)
    PE(lambda e: e.matmul(pb[6][0:1, 0:1], sm(OF["tA"], 1), ONECF[:, 0:1],
                          start=True, stop=True))
    P.w("vector", sPE)
    VD(lambda e: e.tensor_scalar(out=scr(1, 64, 1)[0:1, :], in0=pb[6][0:1, 0:1],
                                 scalar1=1.0 / (2048.0 * 2048.0), scalar2=None,
                                 op0=ALU.mult))
    VD(lambda e: e.tensor_scalar(out=scr(1, 65, 1)[0:1, :], in0=scr(1, 64, 1)[0:1, :],
                                 scalar1=0.5, scalar2=0.5e-10, op0=ALU.mult,
                                 op1=ALU.add))
    VD(lambda e: e.tensor_scalar(out=scr(1, 66, 1)[0:1, :], in0=scr(1, 65, 1)[0:1, :],
                                 scalar1=-1.0, scalar2=None, op0=ALU.mult))
    P.w("tensor", sV)
    PE(lambda e: e.matmul(pb[6][:, 0:3], ONESR[:, :], scr(1, 64, 3)[0:1, :],
                          start=True, stop=True))
    P.w("vector", sPE)
    V(lambda e: e.tensor_copy(out=sm(OF["th"], 3), in_=pb[6][:, 0:3]))
    bank_free[6] = sV.t()
    scP = sm(OF["th"], 3)[:, 0:1]
    thP = sm(OF["th"], 3)[:, 1:2]
    nthP = sm(OF["th"], 3)[:, 2:3]

    def mod_use(L):
        def use(j, tile, ready):
            P.wt("tensor", ready)
            P.w("tensor", sS)
            bank = 6 + (j % 2)
            P.wt("tensor", bank_free[bank])
            for kc in range(DC):
                PE(lambda e, kc=kc, tile=tile, bank=bank, L=L: e.matmul(
                    pb[bank][:, 0:1], tile(kc),
                    sm(OF["scT0"] if L == 0 else OF["scT1"])[:, kc:kc + 1],
                    start=(kc == 0), stop=(kc == DC - 1), skip_group_check=True),
                    inc=(kc == DC - 1))
            pe_t = sPE.t()
            P.wt("vector", pe_t)
            V(lambda e, j=j, bank=bank, L=L: e.tensor_copy(
                out=modsh[:, L * MSHC + j: L * MSHC + j + 1], in_=pb[bank][:, 0:1]))
            bank_free[bank] = sV.t()
            return pe_t
        return use

    stream(wmod, MSHC, mod_use(0), first=0, cast=False)
    stream(wmod, MSHC, mod_use(1), first=MSHC, cast=False)

    P.w("gpsimd", sV)
    P.o("gpsimd", lambda e: e.dma_start(out=cc_mod_i, in_=modsh[:, :]), inc=[(sG, 16)])
    P.w("gpsimd", sG)
    P.o("gpsimd", lambda e: e.collective_compute(
        "AllGather", ALU.bypass, replica_groups=[list(range(NCORE))],
        ins=[cc_mod_i], outs=[cc_mod_o]), inc=[(sG, 1)])
    P.w("gpsimd", sG)
    ccm3 = cc_mod_o.rearrange("(n p) f -> n p f", n=NCORE)
    for j in range(NCORE):
        P.o("gpsimd", lambda e, j=j: e.dma_start(
            out=sm(OF["mc0"], 96)[:, j * MSHC:(j + 1) * MSHC], in_=ccm3[j][:, 0:MSHC]),
            inc=[(sG, 16)])
        P.o("gpsimd", lambda e, j=j: e.dma_start(
            out=sm(OF["mc1"], 96)[:, j * MSHC:(j + 1) * MSHC],
            in_=ccm3[j][:, MSHC:2 * MSHC]), inc=[(sG, 16)])

    P.w("vector", sG)
    for L, (mc, ca1o, gao, cb1o, gbo) in enumerate([
            (OF["mc0"], OF["ca10"], OF["ga0"], OF["cb10"], OF["gb0"]),
            (OF["mc1"], OF["ca11"], OF["ga1"], OF["cb11"], OF["gb1"])]):
        mcol = sm(mc, 96)
        ca_, ga_, cb_, gb_ = mcol[:, 16:32], mcol[:, 32:48], mcol[:, 64:80], mcol[:, 80:96]
        V(lambda e, ca_=ca_, o=ca1o: e.tensor_scalar(
            out=sm(o), in0=ca_, scalar1=1.0, scalar2=None, op0=ALU.add))
        V(lambda e, cb_=cb_, o=cb1o: e.tensor_scalar(
            out=sm(o), in0=cb_, scalar1=1.0, scalar2=None, op0=ALU.add))
        for src_, dsto in ((ga_, gao), (gb_, gbo)):
            VD(lambda e, s=src_: e.tensor_scalar(
                out=scr(1, 0, 16), in0=s, scalar1=0.0, scalar2=None, op0=ALU.is_ge))
            VD(lambda e: e.tensor_scalar(
                out=scr(1, 32, 16), in0=scr(1, 0, 16), scalar1=2.0, scalar2=-1.0,
                op0=ALU.mult, op1=ALU.add))
            VD(lambda e, s=src_: e.tensor_scalar(
                out=scr(1, 16, 16), in0=s, scalar1=-1.0, scalar2=None, op0=ALU.mult))
            VD(lambda e, s=src_: e.tensor_tensor(
                out=scr(1, 48, 16), in0=scr(1, 16, 16), in1=s, op=ALU.max))
            VD(lambda e: e.tensor_scalar(
                out=scr(1, 16, 16), in0=scr(1, 48, 16), scalar1=0.25, scalar2=None,
                op0=ALU.max))
            VD(lambda e, o=dsto: e.tensor_tensor(
                out=sm(o), in0=scr(1, 32, 16), in1=scr(1, 16, 16), op=ALU.mult))
    VD(lambda e: e.reciprocal(out=sm(OF["rgb0"]), in_=sm(OF["gb0"])))
    VD(lambda e: e.reciprocal(out=sm(OF["rgb1"]), in_=sm(OF["gb1"])))
    tap("mod0", lambda: sm(OF["mc0"], 96))
    tap("gclamp", lambda: sm(OF["ca10"], 128))
    tap("mod1", lambda: sm(OF["mc1"], 96))
    tap("thsc", lambda: sm(OF["th"], 3))

    def ln_phase(src_fn, xb_fn, hb_fn, ca1_off, sa_ap, rows_tap=None):
        for bk in (4, 5, 6, 7):
            P.wt("tensor", bank_free[bk])
        for c in range(DC):
            src, tkt = src_fn(c)
            par = (c % 2) * TC
            P.wt("vector", tkt)
            V(lambda e, s=src, c=c: e.tensor_copy(out=xb_fn(c), in_=s))
            P.w("vector", sPE)
            V(lambda e, c=c, par=par: e.tensor_tensor(
                out=scrb(3, par, TC), in0=xb_fn(c), in1=xb_fn(c), op=ALU.mult))
            P.w("tensor", sV)
            for t2 in range(2):
                PE(lambda e, c=c, t2=t2: e.matmul(
                    pb[4 + t2][0:1, :], ONECB[:, 0:1],
                    xb_fn(c)[:, t2 * 512:(t2 + 1) * 512],
                    start=(c == 0), stop=(c == DC - 1),
                    skip_group_check=True), inc=(c == DC - 1))
                PE(lambda e, c=c, t2=t2, par=par: e.matmul(
                    pb[6 + t2][0:1, :], ONECB[:, 0:1],
                    scrb(3, par + t2 * 512, 512), start=(c == 0), stop=(c == DC - 1),
                    skip_group_check=True), inc=(c == DC - 1))
        P.w("vector", sPE)
        for t2 in range(2):
            V(lambda e, t2=t2: e.tensor_scalar(
                out=ROWS[:, t2 * 512:(t2 + 1) * 512], in0=pb[4 + t2][0:1, :],
                scalar1=1.0 / D, scalar2=None, op0=ALU.mult))
            V(lambda e, t2=t2: e.tensor_scalar(
                out=ROWS[:, 1024 + t2 * 512: 1024 + (t2 + 1) * 512],
                in0=pb[6 + t2][0:1, :], scalar1=1.0 / D, scalar2=None, op0=ALU.mult))
        V(lambda e: e.tensor_tensor(out=ROWS[:, 2048:3072], in0=ROWS[:, 0:1024],
                                    in1=ROWS[:, 0:1024], op=ALU.mult))
        V(lambda e: e.tensor_tensor(out=ROWS[:, 1024:2048], in0=ROWS[:, 1024:2048],
                                    in1=ROWS[:, 2048:3072], op=ALU.subtract))
        P.w("scalar", sV)
        Sc(lambda e: e.activation(out=ROWS[:, 2048:3072], in_=ROWS[:, 1024:2048],
                                  func=AF.Ln, bias=cEPS5[0:1, :], scale=1.0))
        Sc(lambda e: e.activation(out=ROWS[:, 1024:2048], in_=ROWS[:, 2048:3072],
                                  func=AF.Exp, scale=-0.5))
        P.w("vector", sS)
        if rows_tap:
            tap(rows_tap, lambda: SCR[0:1, 0:3072])
        V(lambda e: e.tensor_tensor(out=ROWS[:, 2048:3072], in0=ROWS[:, 0:1024],
                                    in1=ROWS[:, 1024:2048], op=ALU.mult))
        V(lambda e: e.tensor_scalar(out=ROWS[:, 2048:3072], in0=ROWS[:, 2048:3072],
                                    scalar1=-1.0, scalar2=None, op0=ALU.mult))
        P.w("tensor", sV)
        P.w("tensor", sS)
        for t2 in range(2):
            PE(lambda e, t2=t2: e.matmul(pb[4 + t2][:, :], ONESR[:, :],
                                         ROWS[:, 1024 + t2 * 512: 1024 + (t2 + 1) * 512],
                                         start=True, stop=True), inc=(t2 == 1))
            PE(lambda e, t2=t2: e.matmul(pb[6 + t2][:, :], ONESR[:, :],
                                         ROWS[:, 2048 + t2 * 512: 2048 + (t2 + 1) * 512],
                                         start=True, stop=True), inc=(t2 == 1))
        P.w("vector", sPE)
        for t2 in range(2):
            V(lambda e, t2=t2: e.tensor_copy(out=scr(0, t2 * 512, 512), in_=pb[4 + t2][:, :]))
            V(lambda e, t2=t2: e.tensor_copy(out=scr(1, t2 * 512, 512), in_=pb[6 + t2][:, :]))
        for bk in (4, 5, 6, 7):
            bank_free[bk] = sV.t()
        for c in range(DC):
            V(lambda e, c=c: e.tensor_tensor(
                out=scr(2), in0=xb_fn(c), in1=scr(0), op=ALU.mult))
            V(lambda e, c=c: e.tensor_tensor(
                out=scr(3), in0=scr(2), in1=scr(1), op=ALU.add))
            V(lambda e, c=c: e.tensor_scalar(
                out=hb_fn(c), in0=scr(3), scalar1=sm(ca1_off)[:, c:c + 1],
                scalar2=sa_ap[:, c:c + 1], op0=ALU.mult, op1=ALU.add))
        return sV.t()

    pbar("p0")
    def h0b_fn(c):
        return Fb[:, c * 2048: c * 2048 + 1024]

    def hloc_fn(c):
        return Fb[:, c * 2048 + 1024: c * 2048 + 2048]

    def x_src(c):
        P.w("sync", sV)
        P.o("sync", lambda e, c=c: e.dma_start(
            out=scr(4), in_=xt[c * 128:(c + 1) * 128, :]), inc=[(sW, 16)])
        return scr(4), sW.t()

    ln_phase(x_src, lambda c: cell(Eb, c), h0b_fn, OF["ca10"],
             sm(OF["mc0"], 96)[:, 0:16])
    tap("h0b", lambda: Fb[:, 0:2 * DC * TC])

    pbar("ln1")
    def gates_use(off, bias_tile):
        def use(ms, tile, ready):
            P.wt("tensor", ready)
            for t2 in range(2):
                bank = (ms % 2) * 2 + t2
                P.wt("tensor", bank_free[bank])
                for kc in range(DC):
                    PE(lambda e, kc=kc, t2=t2, bank=bank, tile=tile: e.matmul(
                        pb[bank][:, :], tile(kc), h0b_fn(kc)[:, t2 * 512:(t2 + 1) * 512],
                        start=(kc == 0), stop=(kc == DC - 1)), inc=(kc == DC - 1))
                pe_t = sPE.t()
                P.wt("scalar", pe_t)
                Sc(lambda e, ms=ms, t2=t2, bank=bank, off=off, bias_tile=bias_tile:
                   e.activation(out=cell(Eb, off + ms, t2 * 512, 512), in_=pb[bank][:, :],
                                func=AF.Sigmoid, bias=bias_tile[:, ms:ms + 1], scale=1.0))
                bank_free[bank] = sS.t()
            return sPE.t()
        return use

    stream(win, DC, gates_use(0, BINP))
    stream(wgate, DC, gates_use(DC, BGP))
    tap("sigig", lambda: Eb[:, 0: DC * TC])
    tap("sigrg", lambda: Eb[:, DC * TC: 2 * DC * TC])

    pbar("gates")
    for c in range(DC):
        P.w("scalar", sV)
        Sc(lambda e, c=c: e.activation(out=scr(0), in_=cell(Eb, DC + c), func=AF.Exp,
                                       scale=sm(OF["sfl"])[:, c:c + 1]))
        Sc(lambda e: e.activation(out=scr(1), in_=scr(0), func=AF.Square))
        Sc(lambda e: e.activation(out=scr(2), in_=scr(1), func=AF.Ln,
                                  bias=cEPS6, scale=-1.0))
        Sc(lambda e: e.activation(out=scr(1), in_=scr(2), func=AF.Exp, scale=0.5))
        P.w("vector", sS)
        V(lambda e, c=c: e.tensor_tensor(out=scr(2), in0=scr(1), in1=cell(Eb, c),
                                         op=ALU.mult))
        V(lambda e, c=c: e.tensor_tensor(out=scr(3), in0=scr(2), in1=h0b_fn(c),
                                         op=ALU.mult))
        V(lambda e: e.tensor_tensor_scan(out=scr(2), data0=scr(0), data1=scr(3),
                                         initial=cZ, op0=ALU.mult, op1=ALU.add))
        V(lambda e, c=c: e.tensor_copy(out=hloc_fn(c), in_=scr(2)))
        V(lambda e, c=c: e.tensor_copy(out=sm(OF["exH0"])[:, c:c + 1],
                                       in_=scr(2, 1023, 1)))
        V(lambda e: e.tensor_tensor_scan(out=scr(3), data0=scr(0), data1=scr(0),
                                         initial=cONE, op0=ALU.mult, op1=ALU.bypass))
        V(lambda e, c=c: e.tensor_copy(out=h0b_fn(c), in_=scr(3)))
        V(lambda e, c=c: e.tensor_copy(out=sm(OF["exA0"])[:, c:c + 1],
                                       in_=scr(3, 1023, 1)))
    tap("hloc0", lambda: Fb[:, 0: 2 * DC * TC])
    tap("sfl", lambda: sm(OF["sfl"], 16))
    tap("dbg", lambda: SM[:, 2976:3072])
    tap("alpha15", lambda: scr(0))
    tap("beta15", lambda: scr(1))

    pbar("scan0")
    def exchange(in_off, width, cc_i, cc_o, pick_off):
        P.w("gpsimd", sV)
        P.w("gpsimd", sS)
        P.o("gpsimd", lambda e: e.dma_start(out=cc_i, in_=sm(in_off, width)),
            inc=[(sG, 16)])
        P.w("gpsimd", sG)
        P.o("gpsimd", lambda e: e.collective_compute(
            "AllGather", ALU.bypass, replica_groups=[list(range(NCORE))],
            ins=[cc_i], outs=[cc_o]), inc=[(sG, 1)])
        cc3 = cc_o.rearrange("(n p) f -> n p f", n=NCORE)
        half = width // 2
        VD(lambda e, half=half, pick_off=pick_off: e.memset(sm(pick_off, half), 0.0))
        VD(lambda e, half=half: e.memset(scr(4, 0, half), 0.0))
        for j in range(NCORE - 1):
            P.w("gpsimd", sG)
            P.w("gpsimd", sV)
            P.o("gpsimd", lambda e, j=j: e.dma_start(
                out=scr(3, 0, width), in_=cc3[j]), inc=[(sG, 16)])
            P.w("vector", sG)
            VD(lambda e, half=half: e.tensor_tensor(
                out=scr(4, half, half), in0=scr(4, 0, half), in1=scr(3, 0, half),
                op=ALU.mult))
            VD(lambda e, half=half: e.tensor_tensor(
                out=scr(4, 0, half), in0=scr(4, half, half), in1=scr(3, half, half),
                op=ALU.add))
            VD(lambda e, j=j, half=half, pick_off=pick_off: e.scalar_tensor_tensor(
                out=sm(pick_off, half), in0=scr(4, 0, half),
                scalar=SELO[:, j + 1:j + 2], in1=sm(pick_off, half),
                op0=ALU.mult, op1=ALU.add))
            if j < 3 and width == 32:
                V(lambda e, j=j: e.tensor_copy(
                    out=SM[:, 2976 + j * 32: 2976 + (j + 1) * 32], in_=scr(3, 0, 32)))

    tap("exa0", lambda: sm(OF["exA0"], 32))
    exchange(OF["exA0"], 32, cc_l0_i, cc_l0_o, OF["carry"])
    VD(lambda e: e.tensor_tensor(out=sm(OF["gac"]), in0=sm(OF["carry"]),
                                 in1=sm(OF["ga0"]), op=ALU.mult))
    tap("carry0", lambda: sm(OF["carry"], 16))
    for c in range(DC):
        par = c % 2
        P.w("sync", sV)
        P.o("sync", lambda e, c=c, par=par: e.dma_start(
            out=scr(par), in_=xt[c * 128:(c + 1) * 128, :]), inc=[(sW, 16)])
        P.w("vector", sW)
        V(lambda e, c=c, par=par: e.scalar_tensor_tensor(
            out=scr(2), in0=h0b_fn(c), scalar=sm(OF["gac"])[:, c:c + 1],
            in1=scr(par), op0=ALU.mult, op1=ALU.add))
        V(lambda e, c=c: e.scalar_tensor_tensor(
            out=cell(F, c), in0=hloc_fn(c), scalar=sm(OF["ga0"])[:, c:c + 1],
            in1=scr(2), op0=ALU.mult, op1=ALU.add))
    tap("x1", lambda: F[:, 0: DC * TC])

    pbar("fix0")
    ln_phase(lambda c: (cell(F, c), None), lambda c: cell(Eb, c),
             lambda c: cell(Eb, DC + c), OF["cb10"], sm(OF["mc0"], 96)[:, 48:64])
    tap("h1b", lambda: Eb[:, DC * TC: 2 * DC * TC])
    for c in range(DC):
        V(lambda e, c=c: e.tensor_scalar(
            out=cell(F, c), in0=cell(F, c), scalar1=sm(OF["rgb0"])[:, c:c + 1],
            scalar2=None, op0=ALU.mult))

    pbar("ln2")
    P.w("tensor", sV)
    for t2 in range(2):
        P.wt("tensor", bank_free[6 + t2])
        for kc in range(DC):
            PE(lambda e, kc=kc, t2=t2: e.matmul(
                pb[6 + t2][0:8, :], GWTB[:, kc * 8:(kc + 1) * 8],
                cell(Eb, DC + kc, t2 * 512, 512), start=(kc == 0), stop=(kc == DC - 1)),
                inc=(kc == DC - 1))
        pe_t = sPE.t()
        P.wt("scalar", pe_t)
        Sc(lambda e, t2=t2: e.activation(out=SC8[:, t2 * 512:(t2 + 1) * 512],
                                         in_=pb[6 + t2][0:8, :], func=AF.Sigmoid))
    for tc8 in range(8):
        P.w("tensor", sS)
        P.w("tensor", sV)
        PE(lambda e, tc8=tc8: e.transpose(pb[6][:, 0:8],
                                          SC8[:, tc8 * 128:(tc8 + 1) * 128],
                                          IDENT[0:8, 0:8]))
        P.w("vector", sPE)
        r = R8
        VD(lambda e: e.tensor_tensor(out=r[:, 0:8], in0=pb[6][:, 0:8], in1=BIAST[:, :],
                                     op=ALU.add))
        VD(lambda e: e.reduce_max(r[:, 8:9], r[:, 0:8], axis=AX.X))
        VD(lambda e: e.tensor_scalar(out=r[:, 16:24], in0=r[:, 0:8], scalar1=r[:, 8:9],
                                     scalar2=None, op0=ALU.is_equal))
        VD(lambda e: e.scalar_tensor_tensor(out=r[:, 24:32], in0=r[:, 16:24],
                                            scalar=-1e30, in1=r[:, 0:8],
                                            op0=ALU.mult, op1=ALU.add))
        VD(lambda e: e.reduce_max(r[:, 9:10], r[:, 24:32], axis=AX.X))
        VD(lambda e: e.tensor_scalar(out=r[:, 32:40], in0=r[:, 24:32], scalar1=r[:, 9:10],
                                     scalar2=None, op0=ALU.is_equal))
        VD(lambda e: e.tensor_tensor(out=r[:, 16:24], in0=r[:, 16:24], in1=r[:, 32:40],
                                     op=ALU.add))
        VD(lambda e: e.tensor_tensor(out=r[:, 24:32], in0=pb[6][:, 0:8], in1=r[:, 16:24],
                                     op=ALU.mult))
        VD(lambda e: e.reduce_sum(r[:, 10:11], r[:, 24:32], axis=AX.X))
        VD(lambda e: e.reciprocal(out=r[:, 11:12], in_=r[:, 10:11]))
        VD(lambda e: e.tensor_scalar(out=r[:, 32:40], in0=r[:, 24:32], scalar1=r[:, 11:12],
                                     scalar2=None, op0=ALU.mult))
        P.w("tensor", sV)
        PE(lambda e, tc8=tc8: e.transpose(pb[7][0:8, 0:128], R8[:, 32:40], IDENT[:, :]))
        P.w("vector", sPE)
        V(lambda e, tc8=tc8: e.tensor_copy(out=WE8[:, tc8 * 128:(tc8 + 1) * 128],
                                           in_=pb[7][0:8, 0:128]))
    bank_free[6] = sV.t()
    bank_free[7] = sV.t()
    P.w("sync", sV)
    P.o("sync", lambda e: e.dma_start(out=we_dram, in_=WE8[:, :]), inc=[(sW, 16)])
    we_dram_t = sW.t()
    tap("we8", lambda: WE8[0:8, :])

    def up_pair_use(kind, e8=None):
        state = {}

        def use(j, tile, ready):
            pair, half = divmod(j, 2)
            parity = pair % 2
            banks = (0, 1, 2, 3) if parity == 0 else (4, 5, 6, 7)
            P.wt("tensor", ready)
            for t2 in range(2):
                bank = banks[half * 2 + t2]
                P.wt("tensor", bank_free[bank])
                for kc in range(DC):
                    PE(lambda e, kc=kc, bank=bank, t2=t2, tile=tile: e.matmul(
                        pb[bank][:, :], tile(kc), cell(Eb, DC + kc, t2 * 512, 512),
                        start=(kc == 0), stop=(kc == DC - 1)), inc=(kc == DC - 1))
            pe_t = sPE.t()
            if half == 0:
                state["first_banks"] = (banks[0], banks[1])
                return pe_t
            yb = state["first_banks"]
            gb_ = (banks[2], banks[3])
            if kind == "shared":
                sil_banks, mul_banks = gb_, yb
            else:
                sil_banks, mul_banks = yb, gb_
            P.wt("scalar", pe_t)
            P.w("scalar", sV)
            for t2 in range(2):
                Sc(lambda e, t2=t2, sb_=sil_banks: e.activation(
                    out=scr(1, t2 * 512, 512), in_=pb[sb_[t2]][:, :], func=AF.Silu))
            s_t = sS.t()
            P.wt("vector", s_t)
            for t2 in range(2):
                if kind == "exp":
                    V(lambda e, t2=t2, mb=mul_banks: e.tensor_tensor(
                        out=scr(2, t2 * 512, 512), in0=pb[mb[t2]][:, :],
                        in1=scr(0, t2 * 512, 512), op=ALU.mult))
                    V(lambda e, t2=t2, pair=pair: e.tensor_tensor(
                        out=cell(Eb, pair, t2 * 512, 512), in0=scr(1, t2 * 512, 512),
                        in1=scr(2, t2 * 512, 512), op=ALU.mult))
                else:
                    V(lambda e, t2=t2, mb=mul_banks, pair=pair: e.tensor_tensor(
                        out=cell(Eb, pair, t2 * 512, 512), in0=scr(1, t2 * 512, 512),
                        in1=pb[mb[t2]][:, :], op=ALU.mult))
            v_t = sV.t()
            for bk in (sil_banks + mul_banks):
                bank_free[bk] = v_t
            return pe_t
        return use

    def down_use(j, tile, ready):
        m = j % DC
        parity = m % 2
        P.wt("tensor", ready)
        for t2 in range(2):
            bank = parity * 2 + t2
            P.wt("tensor", bank_free[bank])
            for kc in range(HES):
                PE(lambda e, kc=kc, bank=bank, t2=t2, tile=tile: e.matmul(
                    pb[bank][:, :], tile(kc), cell(Eb, kc, t2 * 512, 512),
                    start=(kc == 0), stop=(kc == HES - 1)), inc=(kc == HES - 1))
            pe_t = sPE.t()
            P.wt("vector", pe_t)
            V(lambda e, m=m, t2=t2, bank=bank: e.tensor_tensor(
                out=cell(F, m, t2 * 512, 512), in0=cell(F, m, t2 * 512, 512),
                in1=pb[bank][:, :], op=ALU.add))
            bank_free[bank] = sV.t()
        return sPE.t()

    for kh in range(2):
        stream(sgw, 2 * HES, up_pair_use("shared"), first=kh * 2 * HES, kdim=D)
        stream(sdw, DC, down_use, first=kh * DC, kdim=1408)
    tap("om_sh", lambda: F[:, 0: DC * TC])
    for e8 in range(NEXP):
        P.wt("gpsimd", we_dram_t)
        P.w("gpsimd", sV)
        P.o("gpsimd", lambda e, e8=e8: e.dma_start(
            out=scr(0), in_=we_dram[e8:e8 + 1, :].partition_broadcast(128)),
            inc=[(sG, 16)])
        web_t = sG.t()
        P.wt("vector", web_t)
        stream(upw, 2 * HES, up_pair_use("exp", e8), first=e8 * 2 * HES, kdim=D)
        stream(dnw, DC, down_use, first=e8 * DC, kdim=1408)
    for c in range(DC):
        P.w("vector", sPE)
        V(lambda e, c=c: e.tensor_scalar(
            out=cell(F, c), in0=cell(F, c), scalar1=sm(OF["gb0"])[:, c:c + 1],
            scalar2=None, op0=ALU.mult))
    tap("x2", lambda: F[:, 0: DC * TC])

    pbar("ffn0")
    ln_phase(lambda c: (cell(F, c), None), lambda c: cell(Eb, c),
             lambda c: cell(Eb, DC + c), OF["ca11"], sm(OF["mc1"], 96)[:, 0:16])
    for c in range(DC):
        P.o("sync", lambda e, c=c: e.dma_start(out=x2_sp[c], in_=cell(F, c)),
            inc=[(sW, 16)])
    x2sp_t = sW.t()

    gstg_free = [None, None]

    def inpj_use(j, tile, ready):
        P.wt("tensor", ready)
        P.wt("vector", x2sp_t)
        is_gate = j >= DIC
        ms = j - DIC if is_gate else j
        for t2 in range(2):
            bank = (j % 2) * 2 + t2
            P.wt("tensor", bank_free[bank])
            for kc in range(DC):
                PE(lambda e, kc=kc, bank=bank, t2=t2, tile=tile: e.matmul(
                    pb[bank][:, :], tile(kc), cell(Eb, DC + kc, t2 * 512, 512),
                    start=(kc == 0), stop=(kc == DC - 1)), inc=(kc == DC - 1))
            pe_t = sPE.t()
            P.wt("vector", pe_t)
            if is_gate:
                par = ms % 2
                P.wt("vector", gstg_free[par])
                V(lambda e, t2=t2, bank=bank, par=par: e.tensor_copy(
                    out=scrb(par, t2 * 512, 512), in_=pb[bank][:, :]))
            else:
                V(lambda e, ms=ms, t2=t2, bank=bank: e.tensor_copy(
                    out=cell(Fb, ms, t2 * 512, 512), in_=pb[bank][:, :]))
            bank_free[bank] = sV.t()
        if is_gate:
            par = ms % 2
            P.w("sync", sV)
            P.o("sync", lambda e, ms=ms, par=par: e.dma_start(
                out=gate_sp[ms], in_=scrb(par, 0, TC)), inc=[(sW, 16)])
            gstg_free[par] = sW.t()
        return sPE.t()

    stream(inpj, 2 * DIC, inpj_use)
    tap("xmb", lambda: Fb[:, 0: DIC * TC])

    pbar("inpj")
    for c in range(DIC):
        V(lambda e, c=c: e.tensor_copy(out=sm(OF["agb"], 96)[:, c * 3:c * 3 + 3],
                                       in_=cell(Fb, c, 1021, 3)))
    P.w("gpsimd", sV)
    P.o("gpsimd", lambda e: e.dma_start(out=cc_h_i, in_=sm(OF["agb"], 96)),
        inc=[(sG, 16)])
    P.w("gpsimd", sG)
    P.o("gpsimd", lambda e: e.collective_compute(
        "AllGather", ALU.bypass, replica_groups=[list(range(NCORE))],
        ins=[cc_h_i], outs=[cc_h_o]), inc=[(sG, 1)])
    cch3 = cc_h_o.rearrange("(n p) f -> n p f", n=NCORE)
    P.w("vector", sG)
    V(lambda e: e.memset(sm(OF["agb"], 96), 0.0))
    for j in range(NCORE):
        P.w("gpsimd", sG)
        P.w("gpsimd", sV)
        P.o("gpsimd", lambda e, j=j: e.dma_start(out=scr(3, 0, 96), in_=cch3[j]),
            inc=[(sG, 16)])
        P.w("vector", sG)
        VD(lambda e, j=j: e.scalar_tensor_tensor(
            out=sm(OF["agb"], 96), in0=scr(3, 0, 96), scalar=SELP[:, j:j + 1],
            in1=sm(OF["agb"], 96), op0=ALU.mult, op1=ALU.add))
    P.wt("scalar", x2sp_t)
    P.w("scalar", sPE)
    for c in range(DIC):
        P.w("vector", sS)
        V(lambda e, c=c: e.tensor_copy(out=scrb(4, 0, 3),
                                       in_=sm(OF["agb"], 96)[:, c * 3:c * 3 + 3]))
        V(lambda e, c=c: e.tensor_copy(out=scrb(4, 3, TC), in_=cell(Fb, c)))
        V(lambda e, c=c: e.tensor_scalar(
            out=scr(0), in0=scrb(4, 0, TC), scalar1=CW[:, 4 * c:4 * c + 1],
            scalar2=None, op0=ALU.mult))
        for jj in range(1, 4):
            V(lambda e, c=c, jj=jj: e.scalar_tensor_tensor(
                out=scr(0), in0=scrb(4, jj, TC), scalar=CW[:, 4 * c + jj:4 * c + jj + 1],
                in1=scr(0), op0=ALU.mult, op1=ALU.add))
        P.w("scalar", sV)
        Sc(lambda e, c=c: e.activation(out=cell(Eb, c), in_=scr(0), func=AF.Silu,
                                       bias=CB[:, c:c + 1], scale=1.0))
    tap("xtb", lambda: Eb[:, 0: DIC * TC])

    pbar("conv")
    def xp_use(mwidth, banks):
        def use(j, tile, ready):
            kh = j
            P.wt("tensor", ready)
            P.w("tensor", sS)
            for t2 in range(2):
                bank = banks[t2]
                if kh == 0:
                    P.wt("tensor", bank_free[bank])
                for kc in range(DC):
                    PE(lambda e, kc=kc, kh=kh, bank=bank, t2=t2, tile=tile, mw=mwidth:
                       e.matmul(pb[bank][0:mw, :], tile(kc),
                                cell(Eb, kh * DC + kc, t2 * 512, 512),
                                start=(kh == 0 and kc == 0),
                                stop=(kh == 1 and kc == DC - 1)),
                       inc=(kc == DC - 1))
            return sPE.t()
        return use

    stream(xpja, 2, xp_use(128, (0, 1)), kdim=D, mw=128)
    stream(xpjb, 2, xp_use(32, (2, 3)), kdim=D, mw=32)
    pe_t = sPE.t()
    P.wt("vector", pe_t)
    for t2 in range(2):
        V(lambda e, t2=t2: e.tensor_copy(out=scrb(4, t2 * 512, 512), in_=pb[t2][:, :]))
        V(lambda e, t2=t2: e.tensor_copy(out=BCr[:, t2 * 512:(t2 + 1) * 512],
                                         in_=pb[2 + t2][0:32, :]))
    V(lambda e: e.tensor_scalar(out=BCr[0:16, :], in0=BCr[0:16, :], scalar1=-1.0,
                                scalar2=None, op0=ALU.mult))
    for bk in range(4):
        bank_free[bk] = sV.t()
    V(lambda e: e.tensor_copy(out=BCb[:, :], in_=BCr[:, :]))
    P.w("sync", sV)
    P.o("sync", lambda e: e.dma_start(out=bc_dram, in_=BCr[:, :]), inc=[(sW, 16)])
    P.o("sync", lambda e: e.dma_start(out=bcb_dram, in_=BCb[:, :]), inc=[(sW, 16)])
    bc_dram_t = sW.t()
    tap("bcr", lambda: BCr[0:32, :])
    tap("tsb", lambda: scrb(4, 0, TC))

    def dt_use(j, tile, ready):
        m = j
        P.wt("tensor", ready)
        for t2 in range(2):
            bank = (m % 2) * 2 + t2
            P.wt("tensor", bank_free[bank])
            PE(lambda e, bank=bank, t2=t2, tile=tile: e.matmul(
                pb[bank][:, :], tile(0), scrb(4, t2 * 512, 512),
                start=True, stop=True), inc=True)
            pe_t = sPE.t()
            P.wt("scalar", pe_t)
            Sc(lambda e, m=m, t2=t2, bank=bank: e.activation(
                out=scr(m % 4, t2 * 512, 512), in_=pb[bank][:, :], func=AF.Sigmoid,
                bias=sm(OF["ndtb"], 32)[:, m:m + 1], scale=-1.0))
            bank_free[bank] = sS.t()
        if m % 4 == 3:
            for mb in range(m - 3, m + 1):
                Sc(lambda e, mb=mb: e.activation(out=cell(Fb, mb), in_=scr(mb % 4),
                                                 func=AF.Ln))
        return sPE.t()

    stream(dtpj, DIC, dt_use, kdim=128)
    tap("dtn", lambda: Fb[:, 0: DIC * TC])
    P.w("vector", sS)
    for c in range(DIC):
        par = c % 2
        P.w("vector", sW)
        V(lambda e, c=c, par=par: e.tensor_scalar(
            out=scr(par), in0=cell(Eb, c), scalar1=DPT[:, c:c + 1], scalar2=None,
            op0=ALU.mult))
        P.w("sync", sV)
        P.o("sync", lambda e, c=c, par=par: e.dma_start(out=y_sp[c], in_=scr(par)),
            inc=[(sW, 16)])
        V(lambda e, c=c: e.tensor_tensor(out=cell(Eb, c), in0=cell(Fb, c),
                                         in1=cell(Eb, c), op=ALU.mult))
        V(lambda e, c=c: e.reduce_sum(sm(OF["sdt"], 32)[:, c:c + 1], cell(Fb, c),
                                      axis=AX.X))
        P.w("scalar", sV)
        Sc(lambda e, c=c: e.activation(
            out=sm(OF["exAm"], 512)[:, c * 16:(c + 1) * 16],
            in_=sm(OF["negA"], 512)[:, c * 16:(c + 1) * 16], func=AF.Exp,
            scale=sm(OF["sdt"], 32)[:, c:c + 1]))
    tap("ub", lambda: Eb[:, 0: DIC * TC])

    pbar("uprep")
    a_free = [None, None]
    bc_free = [None, None]
    for g in range(DIC // 4):
        cs = list(range(g * 4, g * 4 + 4))
        for c in cs:
            P.w("sync", sV)
            P.w("sync", sW)
            P.o("sync", lambda e, c=c: e.dma_start(out=cell(YG, c % 4),
                                                   in_=y_sp[c]), inc=[(sW, 16)])
        yld_t = sW.t()
        for s in range(NST):
            spar = s % 2
            P.wt("gpsimd", bc_dram_t)
            P.wt("gpsimd", bc_free[spar])
            P.o("gpsimd", lambda e, s=s, spar=spar: e.dma_start(
                out=scrb(3 + spar, 0, TC),
                in_=bcb_dram[s:s + 1, :].partition_broadcast(128)), inc=[(sG, 16)])
            P.o("gpsimd", lambda e, s=s, spar=spar: e.dma_start(
                out=scrb(3 + spar, TC, TC),
                in_=bcb_dram[16 + s:17 + s, :].partition_broadcast(128)), inc=[(sG, 16)])
            bct = sG.t()
            for ci, c in enumerate(cs):
                P.wt("scalar", a_free[0])
                Sc(lambda e, c=c, s=s: e.activation(
                    out=scr(0), in_=cell(Fb, c), func=AF.Exp,
                    scale=sm(OF["negA"], 512)[:, c * 16 + s:c * 16 + s + 1]))
                at = sS.t()
                P.wt("vector", bct)
                P.wt("vector", yld_t)
                V(lambda e, c=c, spar=spar: e.tensor_tensor(
                    out=scrb(1, 0, TC), in0=cell(Eb, c),
                    in1=scrb(3 + spar, 0, TC), op=ALU.mult))
                P.wt("vector", at)
                V(lambda e: e.tensor_tensor_scan(
                    out=scrb(1, TC, TC), data0=scr(0), data1=scrb(1, 0, TC),
                    initial=cZ, op0=ALU.mult, op1=ALU.add))
                V(lambda e, spar=spar: e.tensor_tensor(
                    out=scrb(2, 0, TC), in0=scrb(1, TC, TC),
                    in1=scrb(3 + spar, TC, TC), op=ALU.mult))
                V(lambda e, c=c: e.tensor_tensor(
                    out=cell(YG, c % 4), in0=cell(YG, c % 4), in1=scrb(2, 0, TC),
                    op=ALU.add))
                V(lambda e, c=c, s=s: e.tensor_copy(
                    out=sm(OF["exHm"], 512)[:, c * 16 + s:c * 16 + s + 1],
                    in_=scrb(1, TC + 1023, 1)))
                a_free[0] = sV.t()
            bc_free[spar] = sV.t()
        for c in cs:
            P.w("sync", sV)
            P.o("sync", lambda e, c=c: e.dma_start(out=y_sp[c], in_=cell(YG, c % 4)),
                inc=[(sW, 16)])
    ysp_t = sW.t()
    tap("exhm", lambda: sm(OF["exHm"], 512))

    pbar("scanm")
    exchange(OF["exAm"], 1024, cc_m_i, cc_m_o, OF["carrym"])
    tap("carrym", lambda: sm(OF["carrym"], 512))

    pbar("e3")
    for s in range(NST):
        P.wt("gpsimd", bc_dram_t)
        P.w("gpsimd", sV)
        P.o("gpsimd", lambda e, s=s: e.dma_start(
            out=scr(3), in_=bc_dram[16 + s:17 + s, :].partition_broadcast(128)),
            inc=[(sG, 16)])
        P.w("vector", sG)
        V(lambda e, s=s: e.tensor_copy(out=cell(Eb, s), in_=scr(3)))
    e_free = [None, None]
    for c in range(DIC):
        P.w("vector", sS)
        V(lambda e, c=c: e.tensor_tensor_scan(
            out=scr(0), data0=cell(Fb, c), data1=cell(Fb, c), initial=cZ,
            op0=ALU.add, op1=ALU.bypass))
        V(lambda e: e.memset(scrb(1, 0, TC), 0.0))
        cdt_t = sV.t()
        for s in range(NST):
            epar = s % 2
            P.wt("scalar", e_free[epar])
            P.wt("scalar", cdt_t)
            Sc(lambda e, c=c, s=s, epar=epar: e.activation(
                out=cell(E, 8 + epar), in_=scr(0), func=AF.Exp,
                scale=sm(OF["negA"], 512)[:, c * 16 + s:c * 16 + s + 1]))
            et = sS.t()
            P.wt("vector", et)
            V(lambda e, c=c, s=s, epar=epar: e.scalar_tensor_tensor(
                out=scrb(4, 0, TC), in0=cell(E, 8 + epar),
                scalar=sm(OF["carrym"], 512)[:, c * 16 + s:c * 16 + s + 1],
                in1=cell(Eb, s), op0=ALU.mult, op1=ALU.mult))
            V(lambda e: e.tensor_tensor(out=scrb(1, 0, TC), in0=scrb(1, 0, TC),
                                        in1=scrb(4, 0, TC), op=ALU.add))
            e_free[epar] = sV.t()
        P.w("sync", sV)
        P.w("sync", sS)
        P.wt("sync", ysp_t)
        P.o("sync", lambda e, c=c: e.dma_start(out=scr(2), in_=y_sp[c]), inc=[(sW, 16)])
        P.o("sync", lambda e, c=c: e.dma_start(out=scrb(4, TC, TC), in_=gate_sp[c]),
            inc=[(sW, 16)])
        P.w("vector", sW)
        V(lambda e: e.tensor_tensor(out=scr(2), in0=scr(2), in1=scrb(1, 0, TC),
                                    op=ALU.add))
        P.w("scalar", sW)
        P.w("scalar", sV)
        Sc(lambda e: e.activation(out=scr(3), in_=scrb(4, TC, TC), func=AF.Silu))
        P.w("vector", sS)
        V(lambda e, c=c: e.tensor_tensor(out=cell(Fb, c), in0=scr(2), in1=scr(3),
                                         op=ALU.mult))
    tap("y2b", lambda: Fb[:, 0: DIC * TC])

    pbar("fixm")
    def op_use(j, tile, ready):
        m, kh = divmod(j, 2)
        P.wt("tensor", ready)
        if kh == 0:
            par = m % 2
            P.w("sync", sV)
            P.w("sync", sS)
            P.wt("sync", x2sp_t)
            P.o("sync", lambda e, m=m, par=par: e.dma_start(
                out=scr(par), in_=x2_sp[m]), inc=[(sW, 16)])
        for t2 in range(2):
            bank = (m % 2) * 2 + t2
            if kh == 0:
                P.wt("tensor", bank_free[bank])
            for kc in range(DC):
                PE(lambda e, kc=kc, kh=kh, bank=bank, t2=t2, tile=tile: e.matmul(
                    pb[bank][:, :], tile(kc), cell(Fb, kh * DC + kc, t2 * 512, 512),
                    start=(kh == 0 and kc == 0), stop=(kh == 1 and kc == DC - 1)),
                    inc=(kc == DC - 1))
        if kh == 1:
            pe_t = sPE.t()
            P.wt("vector", pe_t)
            P.w("vector", sW)
            par = m % 2
            for t2 in range(2):
                bank = (m % 2) * 2 + t2
                V(lambda e, m=m, t2=t2, bank=bank, par=par: e.scalar_tensor_tensor(
                    out=cell(E, m, t2 * 512, 512), in0=pb[bank][:, :],
                    scalar=sm(OF["ga1"])[:, m:m + 1],
                    in1=scr(par, t2 * 512, 512), op0=ALU.mult, op1=ALU.add))
                bank_free[bank] = sV.t()
        return sPE.t()

    barrier()
    stream(outpj, 2 * DC, op_use)
    barrier()
    tap("x3", lambda: E[:, 0: DC * TC])

    pbar("outpj")
    ln_phase(lambda c: (cell(E, c), None), lambda c: cell(Fb, c),
             lambda c: cell(Fb, DC + c), OF["cb11"], sm(OF["mc1"], 96)[:, 48:64],
             rows_tap="rows4")
    tap("h3b", lambda: Fb[:, DC * TC: 2 * DC * TC])
    tap("xb4", lambda: Fb[:, 0: DC * TC])

    def tern_prep(j, par, load_t):
        P.wt("vector", load_t)
        P.wt("vector", ws["stgb_free"][par])
        V(lambda e, par=par: e.tensor_scalar(
            out=STGB[par][:, :], in0=STGF[:, :], scalar1=thP, scalar2=None,
            op0=ALU.is_gt))
        V(lambda e, par=par: e.tensor_scalar(
            out=scrb(2, 0, 2048), in0=STGF[:, :], scalar1=nthP, scalar2=None,
            op0=ALU.is_lt))
        V(lambda e, par=par: e.tensor_tensor(
            out=STGB[par][:, :], in0=STGB[par][:, :], in1=scrb(2, 0, 2048),
            op=ALU.subtract))
        return sV.t(), STGB[par]

    def fw2_use(j, tile, ready):
        m = j
        par = m % 2
        P.wt("tensor", ready)
        for t2 in range(2):
            bank = (m % 2) * 2 + t2
            P.wt("tensor", bank_free[bank])
            for kc in range(DC):
                PE(lambda e, kc=kc, bank=bank, t2=t2, tile=tile: e.matmul(
                    pb[bank][:, :], tile(kc), cell(Fb, DC + kc, t2 * 512, 512),
                    start=(kc == 0), stop=(kc == DC - 1)), inc=(kc == DC - 1))
            pe_t = sPE.t()
            P.wt("vector", pe_t)
            P.w("vector", sW)
            V(lambda e, m=m, t2=t2, bank=bank: e.tensor_scalar(
                out=scr(1, t2 * 512, 512), in0=pb[bank][:, :], scalar1=scP,
                scalar2=FBP[:, m:m + 1], op0=ALU.mult, op1=ALU.add))
            V(lambda e, m=m, t2=t2, par=par: e.scalar_tensor_tensor(
                out=scr(3 + (0 if par == 0 else 1), t2 * 512, 512),
                in0=scr(1, t2 * 512, 512),
                scalar=sm(OF["gb1"])[:, m:m + 1], in1=cell(E, m, t2 * 512, 512),
                op0=ALU.mult, op1=ALU.add))
            bank_free[bank] = sV.t()
        P.w("sync", sV)
        P.o("sync", lambda e, m=m, par=par: e.dma_start(
            out=yo[m * 128:(m + 1) * 128, :], in_=scr(3 + (0 if par == 0 else 1))),
            inc=[(sO, 16)])
        return sPE.t()

    stream(fw, DC, fw2_use, prep_fn=tern_prep)

    P.w("gpsimd", sO, sO.n)
    P.w("gpsimd", sW)

    with nc.Block() as block:
        for eng in _Prog.ENGS:
            def make(eng):
                def body(e):
                    for fn in P.streams[eng]:
                        fn(e)
                return body
            getattr(block, eng)(make(eng))
    es.close()
    return nc, tapn


def _host_layout(inp):
    f = np.float32
    asc = np.ascontiguousarray

    def strips(wT, mw=128):
        K, M = wT.shape
        a = wT.reshape(K // 128, 128, M // mw, mw)
        return asc(a.transpose(2, 1, 0, 3).reshape(M // mw, 128, (K // 128) * mw))

    def colP(v):
        n = v.shape[0]
        return asc(v.reshape(n // 128, 128).T)

    x = np.asarray(inp["L0_attn_x"], f)[0]
    win_T = np.asarray(inp["L0_attn_W_in"], f).T
    wg_T = np.asarray(inp["L0_attn_W_gate"], f).T
    sgw_T = np.asarray(inp["L0_ffn_shared_gw"], f).T
    ys = strips(sgw_T[:, :2816])
    gs = strips(sgw_T[:, 2816:])
    sgw_s = np.empty((44, 128, 2048), f)
    sgw_s[0::2] = ys
    sgw_s[1::2] = gs
    sdw_T = np.asarray(inp["L0_ffn_shared_dw"], f).T
    sdw_s = np.empty((32, 128, 1408), f)
    for kh in range(2):
        sdw_s[kh * 16:(kh + 1) * 16] = strips(sdw_T[kh * 1408:(kh + 1) * 1408, :])
    upwf = np.asarray(inp["L0_ffn_up_w"], f)
    upw_s = np.empty((8 * 22, 128, 2048), f)
    for e in range(8):
        uT = upwf[e].T
        gse = strips(uT[:, :1408])
        use_ = strips(uT[:, 1408:])
        upw_s[e * 22 + 0: e * 22 + 22: 2] = gse
        upw_s[e * 22 + 1: e * 22 + 22: 2] = use_
    dnwf = np.asarray(inp["L0_ffn_down_w"], f)
    dnw_s = np.empty((8 * 16, 128, 1408), f)
    for e in range(8):
        dnw_s[e * 16:(e + 1) * 16] = strips(dnwf[e].T)
    inpj_s = strips(np.asarray(inp["L1_attn_in_proj_w"], f).T)
    xpjT = np.asarray(inp["L1_attn_x_proj_w"], f).T
    xpja = np.concatenate([strips(xpjT[:2048, :128]), strips(xpjT[2048:, :128])])
    xpjb = np.concatenate([strips(xpjT[:2048, 128:], mw=32), strips(xpjT[2048:, 128:], mw=32)])
    dtpj_s = strips(np.asarray(inp["L1_attn_dt_proj_w"], f).T)
    opjT = np.asarray(inp["L1_attn_out_proj_w"], f).T
    opj_s = np.empty((32, 128, 2048), f)
    for kh in range(2):
        st_ = strips(opjT[kh * 2048:(kh + 1) * 2048, :])
        for m in range(16):
            opj_s[m * 2 + kh] = st_[m]
    fw_s = strips(np.asarray(inp["L1_ffn_weight"], f).T)
    gwT = np.asarray(inp["L0_ffn_gate_w"], f).T
    gw_t = asc(gwT.reshape(16, 128, 8).transpose(1, 0, 2).reshape(128, 128))
    Af = np.asarray(inp["L1_attn_A"], f)
    AT = asc(Af.reshape(32, 128, 16).transpose(1, 0, 2).reshape(128, 512))
    cwf = np.asarray(inp["L1_attn_conv_w"], f)
    cwT = asc(cwf.reshape(32, 128, 4).transpose(1, 0, 2).reshape(128, 128))
    consts = np.zeros((128, 8), f)
    consts[:, 1] = 1.0
    consts[:, 2] = 1.000001
    consts[:, 3] = 1e-5
    common = dict(
        win=strips(win_T), wgate=strips(wg_T), sgw=sgw_s, sdw=sdw_s,
        upw=upw_s, dnw=dnw_s, inpj=inpj_s, xpja=xpja, xpjb=xpjb,
        dtpj=dtpj_s, outpj=opj_s, fw=fw_s, gw=gw_t,
        cv0=colP(np.asarray(inp["L0_c"], f)[0]),
        cv1=colP(np.asarray(inp["L1_c"], f)[0]),
        flP=colP(np.asarray(inp["L0_attn_forget_lambda"], f)),
        binP=colP(np.asarray(inp["L0_attn_b_in"], f)),
        bgP=colP(np.asarray(inp["L0_attn_b_gate"], f)),
        cwP=cwT, cbP=colP(np.asarray(inp["L1_attn_conv_b"], f)),
        dtbP=colP(np.asarray(inp["L1_attn_dt_proj_b"], f)),
        AT=AT, DPi=colP(np.asarray(inp["L1_attn_D"], f)),
        fbP=colP(np.asarray(inp["L1_ffn_bias"], f)),
        biasT=np.tile(np.asarray(inp["L0_ffn_gate_bias"], f)[None, :], (128, 1)),
        ident=np.eye(128, dtype=f),
        onesr=np.ones((1, 128), f),
        onescb=np.ones((128, 1), ml_dtypes.bfloat16),
        onescf=np.ones((128, 1), f),
        consts=consts,
    )
    wmodf = np.asarray(inp["L0_wmod"], f)
    wmod1f = np.asarray(inp["L1_wmod"], f)
    maps = []
    for i in range(NCORE):
        m = dict(common)
        m["xt"] = asc(x[i * TC:(i + 1) * TC, :].T)
        wm = np.empty((24, 128, 2048), f)
        wm[:12] = strips(wmodf[:, i * 1536:(i + 1) * 1536])
        wm[12:] = strips(wmod1f[:, i * 1536:(i + 1) * 1536])
        m["wmod"] = wm
        so = np.zeros((128, NCORE), f)
        so[:, i] = 1.0
        sp = np.zeros((128, NCORE), f)
        if i > 0:
            sp[:, i - 1] = 1.0
        m["selown"] = so
        m["selprev"] = sp
        maps.append(m)
    return maps


def kernel(**inputs):
    maps = _host_layout(inputs)
    nc, tapn = _build()
    res = run_bass_kernel_spmd(nc, maps, core_ids=list(range(NCORE)))
    outs = [r["yo"] for r in res.results]
    y = np.concatenate([o.T for o in outs], axis=0).reshape(1, T, D)
    if _DEBUG_TAPS:
        kernel.taps = [{k: v for k, v in r.items() if k.startswith("tap_")}
                       for r in res.results]
    return np.ascontiguousarray(y.astype(np.float32))
